# version 05
# baseline (speedup 1.0000x reference)
import numpy as np
from contextlib import ExitStack

import concourse.bass as bass
import concourse.mybir as mybir
from concourse.bass_utils import run_bass_kernel_spmd

F32 = mybir.dt.float32
BF16 = mybir.dt.bfloat16
F16 = mybir.dt.float16
I32 = mybir.dt.int32
AF = mybir.ActivationFunctionType
ALU = mybir.AluOpType
AX = mybir.AxisListType

S = 4096
D = 2048
NCORE = 8

ENGINE_ATTRS = [("g", "gpsimd"), ("t", "tensor"), ("v", "vector"),
                ("s", "scalar"), ("y", "sync")]
BASE_SEMS = ["SV", "SS", "ST", "SG"]


class Ctx:

    def __init__(self, nc):
        self.nc = nc
        self.order = []
        self.sems = {name: 0 for name in BASE_SEMS}
        self.handles = {}
        self._es = ExitStack()

    def val(self, name):
        return self.sems[name]

    def sb(self, name, shape, dtype):
        return self.nc.alloc_sbuf_tensor(name, shape, dtype)

    def ps(self, name, shape, dtype=F32):
        return self._es.enter_context(self.nc.psum_tensor(name, shape, dtype))

    def dram(self, name, shape, dtype, kind="Internal"):
        return self.nc.dram_tensor(name, shape, dtype, kind=kind)

    def op(self, ek, build, inc=None, inc_n=1, waits=()):
        waits = [(s, v) for (s, v) in waits if v]
        for sn, _ in waits:
            self.sems.setdefault(sn, 0)
        target = None
        if inc is not None:
            self.sems.setdefault(inc, 0)
            self.sems[inc] += inc_n
            target = self.sems[inc]

        def closure(eng, build=build, waits=tuple(waits), inc=inc, inc_n=inc_n):
            for sname, v in waits:
                eng.wait_ge(self.handles[sname], v)
            inst = build(eng)
            if inc is not None:
                inst.then_inc(self.handles[inc], inc_n)

        self.order.append((ek, closure))
        return target

    def dma(self, ek, out, in_, inc, waits=()):
        v = self.op(ek, lambda e, out=out, in_=in_: e.dma_start(out=out, in_=in_),
                    inc=inc, inc_n=16, waits=waits)
        return (inc, v)

    def barrier(self):
        snap = tuple((s, v) for s, v in self.sems.items() if v)
        for ek, _ in ENGINE_ATTRS:
            def closure(eng, snap=snap):
                for sname, v in snap:
                    eng.wait_ge(self.handles[sname], v)
            self.order.append((ek, closure))

    def finalize(self, final_waits=None):
        fw = [(s, v) for s, v in self.sems.items() if v]

        def fin(eng, fw=tuple(fw)):
            for sname, v in fw:
                eng.wait_ge(self.handles[sname], v)
        self.order.append(("y", fin))
        for name in self.sems:
            self.handles[name] = self._es.enter_context(self.nc.semaphore(name))
        with self.nc.Block() as block:
            for key, attr in ENGINE_ATTRS:
                ops = [fn for k, fn in self.order if k == key]
                if ops:
                    getattr(block, attr)(
                        lambda eng, ops=tuple(ops): [f(eng) for f in ops])
        self._es.close()



class LNHelper:
    def __init__(self, ctx, tag, alloc=None):
        self.ctx = ctx
        alloc = alloc or ctx.sb
        self.sq = alloc(f"{tag}_lnsq", [128, D], BF16)
        self.st = alloc(f"{tag}_lnst", [128, 8], F32)
        self.rstd = alloc(f"{tag}_rstd", [128, 1], F32)
        self.negmr = alloc(f"{tag}_negmr", [128, 1], F32)
        self.prev = None

    def emit(self, xf_ap, xnb_ap, xwaits, extra_waits=()):
        c = self.ctx
        st = self.st
        w0 = list(xwaits) + list(extra_waits)
        if self.prev:
            w0 = w0 + [("SV", self.prev)]
        vss = c.op("s", lambda e, xf=xf_ap: e.activation(
            self.sq[:], xf, AF.Square, accum_out=st[:, 1:2]), inc="SS", waits=w0)
        vs0 = c.op("v", lambda e, xf=xf_ap: e.reduce_sum(st[:, 0:1], xf, axis=AX.X),
                   inc="SV", waits=w0)
        v1 = c.op("v", lambda e: e.tensor_scalar(st[:, 2:3], st[:, 0:1], 1.0 / D,
                                                 None, op0=ALU.mult),
                  inc="SV", waits=[("SV", vs0)])
        v2 = c.op("v", lambda e: e.tensor_scalar(st[:, 3:4], st[:, 1:2], 1.0 / D,
                                                 1e-5, op0=ALU.mult, op1=ALU.add),
                  inc="SV", waits=[("SS", vss)])
        v3 = c.op("v", lambda e: e.tensor_tensor(st[:, 4:5], st[:, 2:3], st[:, 2:3],
                                                 op=ALU.mult),
                  inc="SV", waits=[("SV", v1)])
        v4 = c.op("v", lambda e: e.scalar_tensor_tensor(st[:, 5:6], st[:, 4:5], -1.0,
                                                        st[:, 3:4], op0=ALU.mult,
                                                        op1=ALU.add),
                  inc="SV", waits=[("SV", v3), ("SV", v2)])
        s1 = c.op("s", lambda e: e.activation(st[:, 6:7], st[:, 5:6], AF.Ln),
                  inc="SS", waits=[("SV", v4)])
        s2 = c.op("s", lambda e: e.activation(self.rstd[:], st[:, 6:7], AF.Exp,
                                              scale=-0.5),
                  inc="SS", waits=[("SS", s1)])
        v5 = c.op("v", lambda e: e.tensor_scalar(self.negmr[:], st[:, 2:3],
                                                 self.rstd[:], -1.0,
                                                 op0=ALU.mult, op1=ALU.mult),
                  inc="SV", waits=[("SS", s2)])
        v6 = c.op("v", lambda e, xf=xf_ap, xn=xnb_ap: e.tensor_scalar(
            xn, xf, self.rstd[:], self.negmr[:], op0=ALU.mult, op1=ALU.add),
            inc="SV", waits=[("SV", v5)])
        self.prev = v6
        return ("SV", v6)



def build_phase_b(nc):
    c = Ctx(nc)
    x_in = nc.dram_tensor("x_in", [S, D], F32, kind="ExternalInput").ap()
    wfcT = nc.dram_tensor("wfcT", [D, 5120], F32, kind="ExternalInput").ap()
    wprojT = nc.dram_tensor("wprojT", [5120, D], F32, kind="ExternalInput").ap()
    ident_in = nc.dram_tensor("ident", [128, 128], BF16, kind="ExternalInput").ap()
    onesf_in = nc.dram_tensor("onesf", [128, 128], F32, kind="ExternalInput").ap()
    x_out = nc.dram_tensor("x_out", [S, D], F32, kind="ExternalOutput").ap()
    emit_phase_b(c, nc, x_in, x_out, wfcT, wprojT, ident_in, onesf_in)
    c.finalize()
    return nc


def emit_phase_b(c, nc, x_in, x_out, wfcT, wprojT, ident_in, onesf_in):
    OC, DC = 40, 16
    TT = 1024
    NST = S // TT
    NSUB = TT // 128
    es = ExitStack()
    def sb(name, shape, dtype):
        return es.enter_context(nc.sbuf_tensor("b_" + name, shape, dtype))
    def ps(name, shape, dtype=F32):
        return es.enter_context(nc.psum_tensor("b_" + name, shape, dtype))

    wfc_t4 = c.dram("wfc_t4", [OC, 128, DC, 128], BF16).ap()
    wproj_t4 = c.dram("wproj_t4", [DC, 128, OC, 128], BF16).ap()

    ident = sb("ident_s", [128, 128], BF16)
    onesf = sb("onesf_s", [128, 128], F32)
    row3 = sb("row3", [1, 4], F32)
    bc3 = sb("bc3", [128, 4], F32)
    sc1 = sb("sc1", [1, 8], F32)
    asum = sb("asum", [128, 64], F32)

    psA = ps("psA", [128, 2, TT], F32)
    pst = ps("pst", [128, 512], BF16)
    pstb = ps("pstb", [128, 512], BF16)
    psm = ps("psm", [1, 8], F32)
    psb3 = ps("psb3", [128, 4], F32)

    c.dma("y", ident[:], ident_in, "Dc")
    dy_const = c.dma("y", onesf[:], onesf_in, "Dc")

    prep_es = ExitStack()
    wbuf = [prep_es.enter_context(nc.sbuf_tensor(f"wbuf{i}", [128, 5120], F32))
            for i in range(2)]
    tern = prep_es.enter_context(nc.sbuf_tensor("tern", [128, 5120], BF16))
    sgn = prep_es.enter_context(nc.sbuf_tensor("sgn", [128, 5120], BF16))
    sqw = prep_es.enter_context(nc.sbuf_tensor("sqw", [128, 5120], F32))

    vlast = [0, 0]
    dgl = [0, 0]
    for pc in range(DC):
        b = pc % 2
        dgl[b] = c.dma("g", wbuf[b][:], wfcT[pc * 128:(pc + 1) * 128, :],
                       f"Dw{b}", waits=[("SV", vlast[b])])
        vlast[b] = c.op("v", lambda e, pc=pc, b=b: e.tensor_reduce(
            asum[:, pc:pc + 1], wbuf[b][:], axis=AX.X, op=ALU.add,
            apply_absolute_value=True), inc="SV", waits=[dgl[b]])
    for pc in range(OC):
        b = pc % 2
        dgl[b] = c.dma("g", wbuf[b][:, 0:2048], wprojT[pc * 128:(pc + 1) * 128, :],
                       f"Dw{b}", waits=[("SV", vlast[b])])
        vlast[b] = c.op("v", lambda e, pc=pc, b=b: e.tensor_reduce(
            asum[:, 16 + pc:17 + pc], wbuf[b][:, 0:2048], axis=AX.X, op=ALU.add,
            apply_absolute_value=True), inc="SV", waits=[dgl[b]])
    v1 = c.op("v", lambda e: e.tensor_reduce(asum[:, 56:57], asum[:, 0:16],
                                             axis=AX.X, op=ALU.add),
              inc="SV", waits=[("SV", vlast[0]), ("SV", vlast[1])])
    v2 = c.op("v", lambda e: e.tensor_reduce(asum[:, 57:58], asum[:, 16:56],
                                             axis=AX.X, op=ALU.add),
              inc="SV", waits=[("SV", v1)])
    t1 = c.op("t", lambda e: e.matmul(psm[:, 0:2], onesf[:, 0:1], asum[:, 56:58],
                                      start=True, stop=True),
              inc="ST", waits=[("SV", v2), dy_const])
    NFC = float(D * 5120)
    v3 = c.op("v", lambda e: e.tensor_scalar(sc1[:, 0:1], psm[:, 0:1], 1.0 / NFC,
                                             1e-5, op0=ALU.mult, op1=ALU.max),
              inc="SV", waits=[("ST", t1)])
    v4 = c.op("v", lambda e: e.tensor_scalar(sc1[:, 1:2], psm[:, 1:2], 1.0 / NFC,
                                             1e-5, op0=ALU.mult, op1=ALU.max),
              inc="SV", waits=[("SV", v3)])
    v5a = c.op("v", lambda e: e.tensor_scalar(sc1[:, 3:4], sc1[:, 0:1], 0.7, None,
                                              op0=ALU.mult),
               inc="SV", waits=[("SV", v4)])
    v5 = c.op("v", lambda e: e.tensor_tensor(row3[:, 0:1], sc1[:, 3:4], sc1[:, 3:4],
                                             op=ALU.mult),
              inc="SV", waits=[("SV", v5a)])
    v6a = c.op("v", lambda e: e.tensor_scalar(sc1[:, 4:5], sc1[:, 1:2], 0.7, None,
                                              op0=ALU.mult),
               inc="SV", waits=[("SV", v5)])
    v6 = c.op("v", lambda e: e.tensor_tensor(row3[:, 1:2], sc1[:, 4:5], sc1[:, 4:5],
                                             op=ALU.mult),
              inc="SV", waits=[("SV", v6a)])
    v7 = c.op("v", lambda e: e.tensor_tensor(sc1[:, 2:3], sc1[:, 0:1], sc1[:, 0:1],
                                             op=ALU.mult),
              inc="SV", waits=[("SV", v6)])
    v8 = c.op("v", lambda e: e.tensor_tensor(row3[:, 2:3], sc1[:, 2:3], sc1[:, 1:2],
                                             op=ALU.mult),
              inc="SV", waits=[("SV", v7)])
    vz3 = c.op("v", lambda e: e.memset(row3[:, 3:4], 0.0), inc="SV",
               waits=[("SV", v8)])
    t2 = c.op("t", lambda e: e.matmul(psb3[:], onesf[0:1, :], row3[:], start=True,
                                      stop=True),
              inc="ST", waits=[("SV", vz3)])
    c.op("v", lambda e: e.tensor_copy(bc3[:], psb3[:]), inc="SV",
         waits=[("ST", t2)])

    c.barrier()

    tn_prev = 0
    gw_prev = ("Dtw", 0)
    for pc in range(DC + OC):
        is_fc = pc < DC
        b = pc % 2
        cols = 5120 if is_fc else 2048
        src = (wfcT[pc * 128:(pc + 1) * 128, :] if is_fc
               else wprojT[(pc - DC) * 128:(pc - DC + 1) * 128, :])
        dgl[b] = c.dma("g", wbuf[b][:, 0:cols], src, f"Dw{b}",
                       waits=[("SV", vlast[b])])
        sg = c.op("s", lambda e, b=b, cols=cols: e.sign(sgn[:, 0:cols],
                                                        wbuf[b][:, 0:cols]),
                  inc="SS", waits=[dgl[b], ("SV", tn_prev)])
        thr = 0 if is_fc else 1
        mk = c.op("v", lambda e, b=b, cols=cols: e.tensor_tensor(
            sqw[:, 0:cols], wbuf[b][:, 0:cols], wbuf[b][:, 0:cols], op=ALU.mult),
            inc="SV", waits=[dgl[b], ("SV", tn_prev)])
        vlast[b] = mk
        tn = c.op("v", lambda e, cols=cols, thr=thr: e.scalar_tensor_tensor(
            tern[:, 0:cols], sqw[:, 0:cols], bc3[:, thr:thr + 1], sgn[:, 0:cols],
            op0=ALU.is_gt, op1=ALU.mult),
            inc="SV", waits=[("SS", sg), ("SV", mk), gw_prev])
        tn_prev = tn
        if is_fc:
            gw_prev = c.dma("g",
                            wfc_t4[:, :, pc, :].rearrange("o p f -> p o f"),
                            tern[:], "Dtw", waits=[("SV", tn)])
        else:
            gw_prev = c.dma("g",
                            wproj_t4[:, :, pc - DC, :].rearrange("d p f -> p d f"),
                            tern[:, 0:2048], "Dtw", waits=[("SV", tn)])

    c.barrier()
    prep_es.close()

    xf = [sb(f"xf{i}", [128, D], F32) for i in range(2)]
    xnb = sb("xnb", [128, D], BF16)
    xnTo = sb("xnTo", [128, DC, TT], BF16)
    wslab = [sb(f"wslab{i}", [128, DC, 128], BF16) for i in range(2)]
    wdslab = [sb(f"wdslab{i}", [128, OC, 128], BF16) for i in range(2)]
    relu_t = sb("relu_t", [128, TT], F32)
    hTb = sb("hTb", [128, OC, TT], BF16)
    xo = [sb(f"xo{i}", [128, D], F32) for i in range(2)]
    ln = LNHelper(c, "b", alloc=sb)

    xf_free = [0, 0]
    xo_free = [("Dso0", 0), ("Dso1", 0)]
    for st in range(NST):
        pst_free = 0
        tlast = 0
        for sub in range(NSUB):
            gi = st * NSUB + sub
            b = sub % 2
            dy = c.dma("y", xf[b][:], x_in[gi * 128:(gi + 1) * 128, :],
                       f"Dx{b}", waits=[("SV", xf_free[b])])
            lnw = ln.emit(xf[b][:], xnb[:], [dy],
                          extra_waits=[("ST", tlast)])
            xf_free[b] = lnw[1]
            for g4 in range(4):
                tv = None
                for i in range(g4 * 4, g4 * 4 + 4):
                    w = [lnw, ("SV", pst_free)] if i == g4 * 4 else ()
                    tv = c.op("t", lambda e, i=i, g4=g4: e.transpose(
                        pst[:, (i - g4 * 4) * 128:(i - g4 * 4 + 1) * 128],
                        xnb[:, i * 128:(i + 1) * 128], ident[:]),
                        inc="ST" if i == g4 * 4 + 3 else None, waits=w)
                tlast = tv
                pst_free = c.op("v", lambda e, g4=g4, sub=sub: e.tensor_copy(
                    xnTo[:, g4 * 4:(g4 + 1) * 4, sub * 128:(sub + 1) * 128],
                    pst[:].rearrange("p (c f) -> p c f", f=128)),
                    inc="SV", waits=[("ST", tv)])
        c.barrier()

        wslab_free = [0, 0]
        psA_free = [0, 0]
        sq_last = 0
        for oc in range(OC):
            b = oc % 2
            dy = c.dma("y", wslab[b][:], wfc_t4[oc], f"Dws{b}",
                       waits=[("ST", wslab_free[b])])
            tv = None
            for dc in range(DC):
                for nt in range(TT // 512):
                    w = ([dy, ("SV", psA_free[b])]
                         if (dc == 0 and nt == 0) else ())
                    tv = c.op("t", lambda e, dc=dc, nt=nt, b=b: e.matmul(
                        psA[:, b, nt * 512:(nt + 1) * 512], wslab[b][:, dc, :],
                        xnTo[:, dc, nt * 512:(nt + 1) * 512],
                        start=(dc == 0), stop=(dc == DC - 1)),
                        inc="ST" if (dc == DC - 1 and nt == TT // 512 - 1) else None,
                        waits=w)
            wslab_free[b] = tv
            rv = c.op("v", lambda e, b=b: e.tensor_scalar(
                relu_t[:], psA[:, b, :], 0.0, None, op0=ALU.max),
                inc="SV", waits=[("ST", tv), ("SS", sq_last)])
            sq_last = c.op("s", lambda e, oc=oc: e.activation(
                hTb[:, oc, :], relu_t[:], AF.Square),
                inc="SS", waits=[("SV", rv)])
            psA_free[b] = rv
        c.barrier()

        wdslab_free = [0, 0]
        psA_free = [0, 0]
        for dc in range(DC):
            b = dc % 2
            dy = c.dma("y", wdslab[b][:], wproj_t4[dc], f"Dwd{b}",
                       waits=[("ST", wdslab_free[b])])
            tv = None
            for oc in range(OC):
                for nt in range(TT // 512):
                    w = ([dy, ("SV", psA_free[b])]
                         if (oc == 0 and nt == 0) else ())
                    tv = c.op("t", lambda e, oc=oc, nt=nt, b=b: e.matmul(
                        psA[:, b, nt * 512:(nt + 1) * 512], wdslab[b][:, oc, :],
                        hTb[:, oc, nt * 512:(nt + 1) * 512],
                        start=(oc == 0), stop=(oc == OC - 1)),
                        inc="ST" if (oc == OC - 1 and nt == TT // 512 - 1) else None,
                        waits=w)
            wdslab_free[b] = tv
            psA_free[b] = c.op("v", lambda e, b=b, dc=dc: e.tensor_scalar(
                xnTo[:, dc, :], psA[:, b, :], bc3[:, 2:3], None, op0=ALU.mult),
                inc="SV", waits=[("ST", tv)])
        c.barrier()

        pstb_free = 0
        for tsub in range(NSUB):
            gi = st * NSUB + tsub
            b = tsub % 2
            dy = c.dma("y", xo[b][:], x_in[gi * 128:(gi + 1) * 128, :],
                       f"Dxo{b}", waits=[xo_free[b]])
            add_last = 0
            for g4 in range(4):
                tv = None
                for i in range(g4 * 4, g4 * 4 + 4):
                    dc = i
                    w = [("SV", pstb_free)] if i == g4 * 4 else ()
                    tv = c.op("t", lambda e, dc=dc, g4=g4, tsub=tsub: e.transpose(
                        pstb[:, (dc - g4 * 4) * 128:(dc - g4 * 4 + 1) * 128],
                        xnTo[:, dc, tsub * 128:(tsub + 1) * 128], ident[:]),
                        inc="ST" if i == g4 * 4 + 3 else None, waits=w)
                add_last = c.op("v", lambda e, g4=g4, b=b: e.tensor_tensor(
                    xo[b][:, g4 * 512:(g4 + 1) * 512], pstb[:],
                    xo[b][:, g4 * 512:(g4 + 1) * 512], op=ALU.add),
                    inc="SV", waits=[("ST", tv), dy])
                pstb_free = add_last
            xo_free[b] = c.dma("y", x_out[gi * 128:(gi + 1) * 128, :], xo[b][:],
                               f"Dso{b}", waits=[("SV", add_last)])
        c.barrier()
    es.close()





def _emit_attn_epilogue(c, a2s, psR, psO, onesf, drecip, rb, aoT):
    h, rc, pv_st = a2s["pend"]
    tb = c.op("t", lambda e: e.matmul(psR[:], onesf[0:1, :], drecip[:],
                                      start=True, stop=True),
              inc="ST", waits=[("SV", rc), ("SV", a2s["psO_free"])])
    rbc = c.op("v", lambda e: e.tensor_copy(rb[:], psR[:]),
               inc="SV", waits=[("ST", tb)])
    a2s["rb_free"] = rbc
    ao = c.op("v", lambda e, h=h: e.tensor_tensor(
        aoT[:, h, :], psO[:], rb[:], op=ALU.mult),
        inc="SV", waits=[("SV", rbc), ("ST", pv_st)])
    a2s["psO_free"] = ao
    a2s["ao_done"] = ao
    a2s["pend"] = None


def build_phase_a(nc, debug=0):
    c = Ctx(nc)
    x_in = nc.dram_tensor("x_in", [S, D], F32, kind="ExternalInput").ap()
    wq_in = nc.dram_tensor("wq", [D, D], F32, kind="ExternalInput").ap()
    wk_in = nc.dram_tensor("wk", [D, 512], F32, kind="ExternalInput").ap()
    wv_in = nc.dram_tensor("wv", [D, 512], F32, kind="ExternalInput").ap()
    wo_in = nc.dram_tensor("wo", [D, D], F32, kind="ExternalInput").ap()
    cos_in = nc.dram_tensor("cosq", [S, 512], F32, kind="ExternalInput").ap()
    sin_in = nc.dram_tensor("sinq", [S, 512], F32, kind="ExternalInput").ap()
    map_in = nc.dram_tensor("map16", [4, 16], F32, kind="ExternalInput").ap()
    ident_in = nc.dram_tensor("identh", [128, 128], F16, kind="ExternalInput").ap()
    identf_in = nc.dram_tensor("identf", [128, 128], F32, kind="ExternalInput").ap()
    onesf_in = nc.dram_tensor("onesf", [128, 128], F32, kind="ExternalInput").ap()
    x_out = nc.dram_tensor("x_out", [S, D], F32, kind="ExternalOutput").ap()
    emit_phase_a(c, nc, x_in, x_out, wq_in, wk_in, wv_in, wo_in, cos_in, sin_in,
                 map_in, ident_in, identf_in, onesf_in, debug=debug)
    c.finalize()
    return nc


def emit_phase_a(c, nc, x_in, x_out, wq_in, wk_in, wv_in, wo_in, cos_in, sin_in,
                 map_in, ident_in, identf_in, onesf_in, debug=0):
    DC = 16
    NH, NKV = 16, 4
    NT = 32
    QT = 512
    NQT = S // QT
    ISR = 0.08838834764831845
    pa_es = ExitStack()
    def sb(name, shape, dtype):
        return pa_es.enter_context(nc.sbuf_tensor("a_" + name, shape, dtype))
    def ps(name, shape, dtype=F32):
        return pa_es.enter_context(nc.psum_tensor("a_" + name, shape, dtype))

    xnb_h = c.dram("xnb_h", [S, D], F16).ap()
    qb_h = c.dram("qb_h", [S, D], F16).ap()

    identh = sb("identh_s", [128, 128], F16)
    identf = sb("identf_s", [128, 128], F32)
    onesf = sb("onesf_s", [128, 128], F32)
    onesh = sb("onesh_s", [128, 1], F16)
    map16 = sb("map16_s", [4, 16], F32)
    kT = sb("kT", [128, NKV, S], F16)
    vst = sb("vst", [128, NT, 512], F16)
    kn2m = sb("kn2m", [128, 4], F32)
    qn2m = sb("qn2m", [128, 16], F32)
    negB = sb("negB", [128, 16], F32)
    sc16 = sb("sc16", [16, 8], F32)
    row16 = sb("row16", [1, 16], F32)

    pst = ps("pst", [128, 512], F16)
    psB = ps("psB", [128, 4, 512], F32)
    psO = ps("psO", [128, 512], F32)
    psDen = ps("psDen", [1, 512], F32)
    psR = ps("psR", [128, 512], F32)

    c.dma("y", identh[:], ident_in, "Dc")
    c.dma("y", identf[:], identf_in, "Dc")
    c.dma("y", onesf[:], onesf_in, "Dc")
    dyc = c.dma("y", map16[:], map_in, "Dc")
    ms = c.op("g", lambda e: e.memset(onesh[:], 1.0), inc="SG")
    z1 = c.op("v", lambda e: e.memset(kn2m[:], 0.0), inc="SV")
    z2 = c.op("v", lambda e: e.memset(qn2m[:], 0.0), inc="SV")
    consts_w = [dyc, ("SG", ms)]

    a1_es = ExitStack()
    wkv = a1_es.enter_context(nc.sbuf_tensor("wkv", [128, DC, 1024], F16))
    xf = [a1_es.enter_context(nc.sbuf_tensor(f"xf{i}", [128, D], F32))
          for i in range(2)]
    xnb = a1_es.enter_context(nc.sbuf_tensor("xnb", [128, D], F16))
    xnT = a1_es.enter_context(nc.sbuf_tensor("xnT", [128, DC, 128], F16))
    ctile = [a1_es.enter_context(nc.sbuf_tensor(f"ctile{i}", [128, 512], F32))
             for i in range(2)]
    stile = [a1_es.enter_context(nc.sbuf_tensor(f"stile{i}", [128, 512], F32))
             for i in range(2)]
    rs = a1_es.enter_context(nc.sbuf_tensor("rs", [128, 4, 128], F32))
    kb = a1_es.enter_context(nc.sbuf_tensor("kb", [128, 512], F16))
    ksq = a1_es.enter_context(nc.sbuf_tensor("ksq", [128, 512], F16))
    kst = a1_es.enter_context(nc.sbuf_tensor("kst", [128, 4], F32))
    ln = LNHelper(c, "a", alloc=lambda n, sh, dt: a1_es.enter_context(
        nc.sbuf_tensor(n, sh, dt)))

    dk = c.dma("g", wkv[:, :, 0:512],
               wk_in.rearrange("(c p) f -> p c f", p=128), "Dwk")
    dv = c.dma("g", wkv[:, :, 512:1024],
               wv_in.rearrange("(c p) f -> p c f", p=128), "Dwk")

    xf_free = [0, 0]
    ct_free = [0, 0]
    pst_free = 0
    tlast = 0
    kv_done = 0
    knm_prev = 0
    rope_read = 0
    vcopy_prev = 0
    xnb_st = ("Dxn", 0)
    NT_A1A = 3 if (debug == 5 or debug >= 10) else NT
    for ti in range(NT_A1A):
        b = ti % 2
        dy = c.dma("y", xf[b][:], x_in[ti * 128:(ti + 1) * 128, :],
                   f"Dx{b}", waits=[("SV", xf_free[b])])
        dct = c.dma("y", ctile[b][:, 0:128], cos_in[ti * 128:(ti + 1) * 128, 0:128],
                    f"Dct{b}", waits=[("SV", ct_free[b])])
        dst_ = c.dma("y", stile[b][:, 0:128], sin_in[ti * 128:(ti + 1) * 128, 0:128],
                     f"Dsn{b}", waits=[("SV", ct_free[b])])
        lnw = ln.emit(xf[b][:], xnb[:], [dy],
                      extra_waits=[("ST", tlast), xnb_st])
        xf_free[b] = lnw[1]
        xnb_st = c.dma("y", xnb_h[ti * 128:(ti + 1) * 128, :], xnb[:], "Dxn",
                       waits=[lnw])
        if 10 <= debug < 11:
            continue
        for g4 in range(4):
            tv = None
            for i in range(g4 * 4, g4 * 4 + 4):
                w = [lnw, ("SV", pst_free), ("ST", kv_done)] if i == g4 * 4 else ()
                if ti == 0 and g4 == 0 and i == 0:
                    w = list(w) + [("Dc", c.val("Dc"))]
                tv = c.op("t", lambda e, i=i, g4=g4: e.transpose(
                    pst[:, (i - g4 * 4) * 128:(i - g4 * 4 + 1) * 128],
                    xnb[:, i * 128:(i + 1) * 128], identh[:]),
                    inc="ST" if i == g4 * 4 + 3 else None, waits=w)
            tlast = tv
            pst_free = c.op("v", lambda e, g4=g4: e.tensor_copy(
                xnT[:, g4 * 4:(g4 + 1) * 4, :],
                pst[:].rearrange("p (c f) -> p c f", f=128)),
                inc="SV", waits=[("ST", tv)])
        if 10 <= debug < 12:
            continue
        tv = None
        for dc in range(DC):
            for n in range(2):
                w = ([("SV", pst_free), dv, ("SV", rope_read),
                      ("SS", vcopy_prev)] if (dc == 0 and n == 0) else ())
                tv = c.op("t", lambda e, dc=dc, n=n: e.matmul(
                    psB[:, n, :], xnT[:, dc, :], wkv[:, dc, n * 512:(n + 1) * 512],
                    start=(dc == 0), stop=(dc == DC - 1)),
                    inc="ST" if (dc == DC - 1 and n == 1) else None, waits=w)
        kv_done = tv
        if 10 <= debug < 13:
            continue
        kv_v = psB[:, 0, :].rearrange("p (g f) -> p g f", f=128)
        cv = ctile[b][:, 0:128].rearrange("p (g f) -> p g f", f=32)
        sv_ = stile[b][:, 0:128].rearrange("p (g f) -> p g f", f=32)
        kbv = kb[:].rearrange("p (g f) -> p g f", f=128)
        r1 = c.op("v", lambda e, kv_v=kv_v, cv=cv: e.tensor_tensor(
            rs[:, :, 0:32], kv_v[:, :, 0:32], cv, op=ALU.mult),
            inc="SV", waits=[("ST", kv_done), dct, dst_, ("SV", knm_prev)])
        r2 = c.op("v", lambda e, kv_v=kv_v, sv_=sv_: e.tensor_tensor(
            rs[:, :, 32:64], kv_v[:, :, 32:64], sv_, op=ALU.mult),
            inc="SV", waits=())
        if debug == 131:
            rope_read = r2
            ct_free[b] = r2
            continue
        r3 = c.op("v", lambda e, kv_v=kv_v, sv_=sv_: e.tensor_tensor(
            rs[:, :, 64:96], kv_v[:, :, 0:32], sv_, op=ALU.mult),
            inc="SV", waits=())
        r4 = c.op("v", lambda e, kv_v=kv_v, cv=cv: e.tensor_tensor(
            rs[:, :, 96:128], kv_v[:, :, 32:64], cv, op=ALU.mult),
            inc="SV", waits=())
        if debug == 132:
            rope_read = r4
            ct_free[b] = r4
            continue
        r5 = c.op("v", lambda e, kbv=kbv: e.tensor_sub(
            kbv[:, :, 0:32], rs[:, :, 0:32], rs[:, :, 32:64]),
            inc="SV", waits=[("SV", r2)])
        r6 = c.op("v", lambda e, kbv=kbv: e.tensor_add(
            kbv[:, :, 32:64], rs[:, :, 64:96], rs[:, :, 96:128]),
            inc="SV", waits=[("SV", r4)])
        if debug == 133:
            rope_read = r6
            ct_free[b] = r6
            continue
        r7 = c.op("v", lambda e, kbv=kbv, kv_v=kv_v: e.tensor_copy(
            kbv[:, :, 64:128], kv_v[:, :, 64:128]),
            inc="SV", waits=[("ST", kv_done), ("SV", knm_prev)])
        ct_free[b] = r6
        rope_read = r6
        if 10 <= debug < 14 or debug in (131, 132, 133):
            continue
        vcopy_prev = c.op("s", lambda e, ti=ti: e.copy(vst[:, ti, :], psB[:, 1, :]),
                          inc="SS", waits=[("SV", r7)])
        n1 = c.op("v", lambda e: e.tensor_mul(ksq[:], kb[:], kb[:]),
                  inc="SV", waits=[("SV", r5), ("SV", r6), ("SV", r7)])
        n2 = c.op("v", lambda e: e.tensor_reduce(
            kst[:], ksq[:].rearrange("p (g f) -> p g f", f=128), axis=AX.X,
            op=ALU.add), inc="SV", waits=[("SV", n1)])
        knm_prev = c.op("v", lambda e: e.tensor_max(kn2m[:], kn2m[:], kst[:]),
                        inc="SV", waits=[("SV", n2), ("SV", z1)])
        if 10 <= debug < 15:
            continue
        tv = None
        for i in range(4):
            w = ([("SV", r5), ("SV", r6), ("SV", r7), ("SV", pst_free)]
                 if i == 0 else ())
            tv = c.op("t", lambda e, i=i: e.transpose(
                pst[:, i * 128:(i + 1) * 128], kb[:, i * 128:(i + 1) * 128],
                identh[:]), inc="ST" if i == 3 else None, waits=w)
        pst_free = c.op("v", lambda e, ti=ti: e.tensor_copy(
            kT[:, :, ti * 128:(ti + 1) * 128],
            pst[:].rearrange("p (g f) -> p g f", f=128)),
            inc="SV", waits=[("ST", tv)])
        tlast = tv
    c.barrier()
    a1_es.close()

    if debug in (2, 5) or debug >= 10:
        pa_es.close()
        return

    a1b_es = ExitStack()
    wqs = a1b_es.enter_context(nc.sbuf_tensor("wqs", [128, DC, 2048], F16))
    xnbt = [a1b_es.enter_context(nc.sbuf_tensor(f"xnbt{i}", [128, D], F16))
            for i in range(2)]
    xnTb = a1b_es.enter_context(nc.sbuf_tensor("xnTb", [128, DC, 128], F16))
    ctile = [a1b_es.enter_context(nc.sbuf_tensor(f"ctileb{i}", [128, 512], F32))
             for i in range(2)]
    stile = [a1b_es.enter_context(nc.sbuf_tensor(f"stileb{i}", [128, 512], F32))
             for i in range(2)]
    rsb = a1b_es.enter_context(nc.sbuf_tensor("rsb", [128, 16, 128], F32))
    qb = a1b_es.enter_context(nc.sbuf_tensor("qb", [128, D], F16))
    qsq = a1b_es.enter_context(nc.sbuf_tensor("qsq", [128, D], F16))
    qst = a1b_es.enter_context(nc.sbuf_tensor("qst", [128, 16], F32))

    dq = c.dma("g", wqs[:], wq_in.rearrange("(c p) f -> p c f", p=128), "Dwq")
    xn_free = [0, 0]
    ct_free = [0, 0]
    pst_free = 0
    q_done = 0
    qnm_prev = 0
    rope_read = 0
    qb_store = ("Dqb", 0)
    for ti in range(NT):
        b = ti % 2
        dxn = c.dma("y", xnbt[b][:], xnb_h[ti * 128:(ti + 1) * 128, :],
                    f"Dxq{b}", waits=[("ST", xn_free[b])])
        dct = c.dma("y", ctile[b][:], cos_in[ti * 128:(ti + 1) * 128, :],
                    f"Dct{b}", waits=[("SV", ct_free[b])])
        dst_ = c.dma("y", stile[b][:], sin_in[ti * 128:(ti + 1) * 128, :],
                     f"Dsn{b}", waits=[("SV", ct_free[b])])
        for g4 in range(4):
            tv = None
            for i in range(g4 * 4, g4 * 4 + 4):
                w = ([dxn, ("SV", pst_free), ("ST", q_done)]
                     if i == g4 * 4 else ())
                tv = c.op("t", lambda e, i=i, g4=g4, b=b: e.transpose(
                    pst[:, (i - g4 * 4) * 128:(i - g4 * 4 + 1) * 128],
                    xnbt[b][:, i * 128:(i + 1) * 128], identh[:]),
                    inc="ST" if i == g4 * 4 + 3 else None, waits=w)
            xn_free[b] = tv
            pst_free = c.op("v", lambda e, g4=g4: e.tensor_copy(
                xnTb[:, g4 * 4:(g4 + 1) * 4, :],
                pst[:].rearrange("p (c f) -> p c f", f=128)),
                inc="SV", waits=[("ST", tv)])
        tv = None
        for dc in range(DC):
            for n in range(4):
                w = ([("SV", pst_free), dq, ("SV", rope_read)]
                     if (dc == 0 and n == 0) else ())
                tv = c.op("t", lambda e, dc=dc, n=n: e.matmul(
                    psB[:, n, :], xnTb[:, dc, :], wqs[:, dc, n * 512:(n + 1) * 512],
                    start=(dc == 0), stop=(dc == DC - 1)),
                    inc="ST" if (dc == DC - 1 and n == 3) else None, waits=w)
        q_done = tv
        qv = psB[:].rearrange("p n (h f) -> p (n h) f", f=128)
        cv = ctile[b][:].rearrange("p (h f) -> p h f", f=32)
        sv_ = stile[b][:].rearrange("p (h f) -> p h f", f=32)
        qbv = qb[:].rearrange("p (h f) -> p h f", f=128)
        r1 = c.op("v", lambda e, qv=qv, cv=cv: e.tensor_tensor(
            rsb[:, :, 0:32], qv[:, :, 0:32], cv, op=ALU.mult),
            inc="SV", waits=[("ST", q_done), dct, dst_, ("SV", qnm_prev),
                             qb_store])
        r2 = c.op("v", lambda e, qv=qv, sv_=sv_: e.tensor_tensor(
            rsb[:, :, 32:64], qv[:, :, 32:64], sv_, op=ALU.mult),
            inc="SV", waits=())
        r3 = c.op("v", lambda e, qv=qv, sv_=sv_: e.tensor_tensor(
            rsb[:, :, 64:96], qv[:, :, 0:32], sv_, op=ALU.mult),
            inc="SV", waits=())
        r4 = c.op("v", lambda e, qv=qv, cv=cv: e.tensor_tensor(
            rsb[:, :, 96:128], qv[:, :, 32:64], cv, op=ALU.mult),
            inc="SV", waits=())
        r5 = c.op("v", lambda e, qbv=qbv: e.tensor_sub(
            qbv[:, :, 0:32], rsb[:, :, 0:32], rsb[:, :, 32:64]),
            inc="SV", waits=[("SV", r2)])
        r6 = c.op("v", lambda e, qbv=qbv: e.tensor_add(
            qbv[:, :, 32:64], rsb[:, :, 64:96], rsb[:, :, 96:128]),
            inc="SV", waits=[("SV", r4)])
        r7 = c.op("v", lambda e, qbv=qbv, qv=qv: e.tensor_copy(
            qbv[:, :, 64:128], qv[:, :, 64:128]),
            inc="SV", waits=[("ST", q_done), qb_store, ("SV", qnm_prev)])
        ct_free[b] = r6
        rope_read = r6
        n1 = c.op("v", lambda e: e.tensor_mul(qsq[:], qb[:], qb[:]),
                  inc="SV", waits=[("SV", r5), ("SV", r6), ("SV", r7)])
        n2 = c.op("v", lambda e: e.tensor_reduce(
            qst[:], qsq[:].rearrange("p (h f) -> p h f", f=128), axis=AX.X,
            op=ALU.add), inc="SV", waits=[("SV", n1)])
        qnm_prev = c.op("v", lambda e: e.tensor_max(qn2m[:], qn2m[:], qst[:]),
                        inc="SV", waits=[("SV", n2), ("SV", z2)])
        qb_store = c.dma("y", qb_h[ti * 128:(ti + 1) * 128, :], qb[:], "Dqb",
                         waits=[("SV", qnm_prev)])
    c.barrier()
    a1b_es.close()

    if debug == 3:
        pa_es.close()
        return

    t1 = c.op("t", lambda e: e.transpose(psR[0:16, 0:128], qn2m[:], identf[:]),
              inc="ST", waits=consts_w)
    v1 = c.op("v", lambda e: e.reduce_max(sc16[:, 0:1], psR[0:16, 0:128],
                                          axis=AX.X),
              inc="SV", waits=[("ST", t1)])
    t2 = c.op("t", lambda e: e.transpose(psR[0:4, 0:128], kn2m[:], identf[:]),
              inc="ST", waits=[("SV", v1)])
    v2 = c.op("v", lambda e: e.reduce_max(sc16[0:4, 1:2], psR[0:4, 0:128],
                                          axis=AX.X),
              inc="SV", waits=[("ST", t2)])
    t3 = c.op("t", lambda e: e.matmul(psR[0:16, 0:1], map16[:], sc16[0:4, 1:2],
                                      start=True, stop=True),
              inc="ST", waits=[("SV", v2)])
    v3 = c.op("v", lambda e: e.tensor_copy(sc16[:, 2:3], psR[0:16, 0:1]),
              inc="SV", waits=[("ST", t3)])
    v4 = c.op("v", lambda e: e.tensor_tensor(sc16[:, 3:4], sc16[:, 0:1],
                                             sc16[:, 2:3], op=ALU.mult),
              inc="SV", waits=[("SV", v3)])
    s1 = c.op("s", lambda e: e.activation(sc16[:, 4:5], sc16[:, 3:4], AF.Ln),
              inc="SS", waits=[("SV", v4)])
    s2 = c.op("s", lambda e: e.activation(sc16[:, 5:6], sc16[:, 4:5], AF.Exp,
                                          scale=0.5),
              inc="SS", waits=[("SS", s1)])
    v5 = c.op("v", lambda e: e.tensor_scalar(sc16[:, 6:7], sc16[:, 5:6], -ISR,
                                             None, op0=ALU.mult),
              inc="SV", waits=[("SS", s2)])
    t4 = c.op("t", lambda e: e.transpose(psR[0:1, 0:16], sc16[:, 6:7], identf[0:16, 0:16]),
              inc="ST", waits=[("SV", v5)])
    v6 = c.op("v", lambda e: e.tensor_copy(row16[:], psR[0:1, 0:16]),
              inc="SV", waits=[("ST", t4)])
    t5 = c.op("t", lambda e: e.matmul(psR[:, 0:16], onesf[0:1, :], row16[:],
                                      start=True, stop=True),
              inc="ST", waits=[("SV", v6)])
    v7 = c.op("v", lambda e: e.tensor_copy(negB[:], psR[:, 0:16]),
              inc="SV", waits=[("ST", t5)])
    negb_w = ("SV", v7)
    c.barrier()

    if debug == 1:
        pa_es.close()
        return

    a2_es = ExitStack()
    wos = a2_es.enter_context(nc.sbuf_tensor("wos", [128, DC, 2048], F16))
    qT = a2_es.enter_context(nc.sbuf_tensor("qT", [128, NH, QT], F16))
    qbl = [a2_es.enter_context(nc.sbuf_tensor(f"qbl{i}", [128, D], F16))
           for i in range(2)]
    Pb = a2_es.enter_context(nc.sbuf_tensor("Pb", [128, 2, 1024], F16))
    u2 = a2_es.enter_context(nc.sbuf_tensor("u2", [128, 2, 512], F16))
    dsum = a2_es.enter_context(nc.sbuf_tensor("dsum", [128, 8, 512], F16))
    densb = a2_es.enter_context(nc.sbuf_tensor("densb", [1, 512], F32))
    drecip = a2_es.enter_context(nc.sbuf_tensor("drecip", [1, 512], F32))
    dscr = a2_es.enter_context(nc.sbuf_tensor("dscr", [1, 512], F32))
    rb = a2_es.enter_context(nc.sbuf_tensor("rb", [128, 512], F32))
    aoT = a2_es.enter_context(nc.sbuf_tensor("aoT", [128, NH, QT], F16))
    xr = [a2_es.enter_context(nc.sbuf_tensor(f"xr{i}", [128, D], F32))
          for i in range(2)]

    dwo = c.dma("g", wos[:], wo_in.rearrange("(c p) f -> p c f", p=128), "Dwo")

    xr_free = [("Dso0", 0), ("Dso1", 0)]
    qbl_free = [0, 0]
    pst_free = 0
    wo_done = 0
    add4 = 0
    a2s = dict(pend=None, exp_free=[(0, 0), (0, 0)], prev_exp=[0, 0],
               psO_free=0, den_free=0, rb_free=0, u_free=0, dsum_free=0,
               ao_done=0, add4_prev=0)
    for qt in range(NQT):
        qT_ready = 0
        for sub in range(4):
            b = sub % 2
            gi = qt * 4 + sub
            dq2 = c.dma("y", qbl[b][:], qb_h[gi * 128:(gi + 1) * 128, :],
                        f"Dq2{b}", waits=[("ST", qbl_free[b])])
            for g4 in range(4):
                tv = None
                for i in range(g4 * 4, g4 * 4 + 4):
                    w = ([dq2, ("SV", pst_free), ("ST", wo_done)]
                         if i == g4 * 4 else ())
                    tv = c.op("t", lambda e, i=i, g4=g4, b=b: e.transpose(
                        pst[:, (i - g4 * 4) * 128:(i - g4 * 4 + 1) * 128],
                        qbl[b][:, i * 128:(i + 1) * 128], identh[:]),
                        inc="ST" if i == g4 * 4 + 3 else None, waits=w)
                qbl_free[b] = tv
                pst_free = c.op("v", lambda e, g4=g4, sub=sub: e.tensor_copy(
                    qT[:, g4 * 4:(g4 + 1) * 4, sub * 128:(sub + 1) * 128],
                    pst[:].rearrange("p (c f) -> p c f", f=128)),
                    inc="SV", waits=[("ST", tv)])
            qT_ready = pst_free
        for pp0 in [0]:
            pass
        for h in range(NH):
            g = h // 4
            dsv_last = 0
            for pp in range(16):
                pb = pp % 2
                mm2 = None
                for half in range(2):
                    ci = pp * 2 + half
                    w = []
                    if half == 0:
                        if pp == 0 and h == 0:
                            w += [("SV", qT_ready), negb_w,
                                  ("SV", a2s["add4_prev"])]
                        w.append(("SS", a2s["prev_exp"][pb]))
                    mm2 = c.op("t", lambda e, ci=ci, pb=pb, half=half, h=h, g=g:
                               e.matmul(psB[:, 2 * pb + half, :],
                                        kT[:, g, ci * 128:(ci + 1) * 128],
                                        qT[:, h, :], start=True, stop=True),
                               inc="ST" if half == 1 else None, waits=w)
                if pp == 0 and a2s["pend"] is not None:
                    _emit_attn_epilogue(c, a2s, psR, psO, onesf, drecip, rb, aoT)
                ex = c.op("s", lambda e, pb=pb, h=h: e.activation(
                    Pb[:, pb, :],
                    psB[:, 2 * pb:2 * pb + 2, :].rearrange("p a f -> p (a f)"),
                    AF.Exp, bias=negB[:, h:h + 1], scale=ISR),
                    inc="SS",
                    waits=[("ST", mm2), ("ST", a2s["exp_free"][pb][0]),
                           ("SV", a2s["exp_free"][pb][1])])
                a2s["prev_exp"][pb] = ex
                pv2 = None
                for half in range(2):
                    ci = pp * 2 + half
                    w = []
                    if half == 0:
                        w = [("SS", ex)]
                        if pp == 0:
                            w.append(("SV", a2s["psO_free"]))
                    pv2 = c.op("t", lambda e, ci=ci, pb=pb, half=half, g=g:
                               e.matmul(psO[:],
                                        vst[:, ci, g * 128:(g + 1) * 128],
                                        Pb[:, pb, half * 512:(half + 1) * 512],
                                        start=(ci == 0), stop=(ci == 31)),
                               inc="ST" if (half == 1) else None, waits=w)
                uo = c.op("v", lambda e, pb=pb: e.tensor_add(
                    u2[:, pb, :], Pb[:, pb, 0:512], Pb[:, pb, 512:1024]),
                    inc="SV", waits=[("SS", ex), ("SV", a2s["u_free"])])
                a2s["exp_free"][pb] = (pv2, uo)
                if pb == 1:
                    grp = pp // 2
                    dsv = c.op("v", lambda e, grp=grp: e.tensor_add(
                        dsum[:, grp, :], u2[:, 0, :], u2[:, 1, :]),
                        inc="SV", waits=[("SV", uo), ("ST", a2s["dsum_free"])])
                    a2s["u_free"] = dsv
                    dsv_last = dsv
            den_mm = 0
            for grp in range(8):
                w = []
                if grp == 0:
                    w = [("SV", dsv_last), ("SV", a2s["den_free"])]
                den_mm = c.op("t", lambda e, grp=grp: e.matmul(
                    psDen[:], onesh[:], dsum[:, grp, :],
                    start=(grp == 0), stop=(grp == 7)),
                    inc="ST" if grp == 7 else None, waits=w)
            a2s["dsum_free"] = den_mm
            dcp = c.op("v", lambda e: e.tensor_copy(densb[:], psDen[:]),
                       inc="SV", waits=[("ST", den_mm)])
            a2s["den_free"] = dcp
            rc = c.op("v", lambda e: e.reciprocal(drecip[:], densb[:]),
                      inc="SV", waits=[("SV", dcp)])
            a2s["pend"] = (h, rc, c.val("ST"))
        _emit_attn_epilogue(c, a2s, psR, psO, onesf, drecip, rb, aoT)
        ao_done = a2s["ao_done"]
        for sub in range(4):
            b = sub % 2
            gi = qt * 4 + sub
            dxr = c.dma("y", xr[b][:], x_in[gi * 128:(gi + 1) * 128, :],
                        f"Dxr{b}", waits=[xr_free[b]])
            tv = None
            for hc in range(DC):
                for n in range(4):
                    w = []
                    if hc == 0 and n == 0:
                        w = [("SV", ao_done), dwo]
                        if add4:
                            w.append(("SV", add4))
                    tv = c.op("t", lambda e, hc=hc, n=n, sub=sub: e.matmul(
                        psB[:, n, :], aoT[:, hc, sub * 128:(sub + 1) * 128],
                        wos[:, hc, n * 512:(n + 1) * 512],
                        start=(hc == 0), stop=(hc == DC - 1)),
                        inc="ST" if (hc == DC - 1 and n == 3) else None, waits=w)
            wo_last = tv
            for n in range(4):
                add4 = c.op("v", lambda e, n=n, b=b: e.tensor_tensor(
                    xr[b][:, n * 512:(n + 1) * 512], psB[:, n, :],
                    xr[b][:, n * 512:(n + 1) * 512], op=ALU.add),
                    inc="SV", waits=([("ST", wo_last), dxr]
                                     if n == 0 else ()))
            xr_free[b] = c.dma("y", x_out[gi * 128:(gi + 1) * 128, :], xr[b][:],
                               f"Dso{b}", waits=[("SV", add4)])
        wo_done = wo_last
        a2s["add4_prev"] = add4
        c.barrier()
    a2_es.close()
    pa_es.close()


def _rope_tables():
    inv = (1.0 / (10000.0 ** (np.arange(0, 64, 2, dtype=np.float32)
                              / np.float32(64)))).astype(np.float32)
    ang = np.arange(S, dtype=np.float32)[:, None] * inv[None, :]
    cos = np.cos(ang).astype(np.float32)
    sin = np.sin(ang).astype(np.float32)
    return np.tile(cos, (1, 16)), np.tile(sin, (1, 16))


def _perm_cols(w, nheads):
    perm = np.concatenate([np.arange(0, 64, 2), np.arange(1, 64, 2),
                           np.arange(64, 128)])
    w = np.asarray(w, dtype=np.float32).copy()
    out = np.empty_like(w)
    for h in range(nheads):
        out[:, h * 128:(h + 1) * 128] = w[:, h * 128 + perm]
    return out


def run_phase_a(x, Wq, Wk, Wv, Wo, debug=0):
    nc = bass.Bass("TRN2", target_bir_lowering=False)
    build_phase_a(nc, debug=debug)
    wq = np.ascontiguousarray(_perm_cols(Wq, 16))
    wk = np.ascontiguousarray(_perm_cols(Wk, 4))
    wv = np.ascontiguousarray(np.asarray(Wv, dtype=np.float32))
    wo = np.ascontiguousarray(np.asarray(Wo, dtype=np.float32))
    cosq, sinq = _rope_tables()
    map16 = np.zeros((4, 16), np.float32)
    for h in range(16):
        map16[h // 4, h] = 1.0
    in_maps = [dict(x_in=np.ascontiguousarray(x[i]), wq=wq, wk=wk, wv=wv, wo=wo,
                    cosq=cosq, sinq=sinq, map16=map16, identh=_ident_f16(),
                    identf=np.eye(128, dtype=np.float32),
                    onesf=_onesf()) for i in range(NCORE)]
    res = run_bass_kernel_spmd(nc, in_maps, core_ids=list(range(NCORE)))
    return np.stack([r["x_out"] for r in res.results], axis=0)



def build_phase_c(nc):
    c = Ctx(nc)
    x_in = nc.dram_tensor("x_in", [S, D], F32, kind="ExternalInput").ap()
    w_hg = nc.dram_tensor("w_hg", [D, 6144], F32, kind="ExternalInput").ap()
    w_out = nc.dram_tensor("w_out", [3072, D], F32, kind="ExternalInput").ap()
    ident_in = nc.dram_tensor("ident", [128, 128], F16, kind="ExternalInput").ap()
    x_out = nc.dram_tensor("x_out", [S, D], F32, kind="ExternalOutput").ap()
    emit_phase_c(c, nc, x_in, x_out, w_hg, w_out, ident_in)
    c.finalize()
    return nc


def emit_phase_c(c, nc, x_in, x_out, w_hg, w_out, ident_in):
    EC = 48
    ECH = 24
    DC = 16
    TT = 1024
    NST = S // TT
    NSUB = TT // 128
    es = ExitStack()
    def sb(name, shape, dtype):
        return es.enter_context(nc.sbuf_tensor("c_" + name, shape, dtype))
    def ps(name, shape, dtype=F32):
        return es.enter_context(nc.psum_tensor("c_" + name, shape, dtype))

    whg_t4 = c.dram("whg_t4", [EC, 128, DC, 128], F16).ap()
    wout_t4 = c.dram("wout_t4", [DC, 128, ECH, 128], F16).ap()

    ident = sb("ident_s", [128, 128], F16)
    pst = ps("pst", [128, 512], F16)
    pstb = ps("pstb", [128, 512], F16)
    psC = ps("psC", [128, 2, TT], F32)

    c.dma("y", ident[:], ident_in, "cDc")

    prep_es = ExitStack()
    wrow = [prep_es.enter_context(nc.sbuf_tensor(f"c_wrow{i}", [128, 6144], F16))
            for i in range(2)]
    gw = [("Dpr0", 0), ("Dpr1", 0)]
    for dc in range(DC):
        b = dc % 2
        dl = c.dma("g", wrow[b][:], w_hg[dc * 128:(dc + 1) * 128, :], f"Dpr{b}",
                   waits=[gw[b]])
        gw[b] = c.dma("g", whg_t4[:, :, dc, :].rearrange("e p f -> p e f"),
                      wrow[b][:], f"Dpr{b}", waits=[dl])
    for ec in range(ECH):
        b = ec % 2
        dl = c.dma("g", wrow[b][:, 0:2048], w_out[ec * 128:(ec + 1) * 128, :],
                   f"Dpr{b}", waits=[gw[b]])
        gw[b] = c.dma("g", wout_t4[:, :, ec, :].rearrange("d p f -> p d f"),
                      wrow[b][:, 0:2048], f"Dpr{b}", waits=[dl])
    c.barrier()
    prep_es.close()

    xf = [sb(f"xf{i}", [128, D], F32) for i in range(2)]
    xnb = sb("xnb", [128, D], F16)
    xnTo = sb("xnTo", [128, DC, TT], F16)
    hb = sb("hb", [128, ECH, TT], F16)
    wgslab = [sb(f"wgslab{i}", [128, DC, 128], F16) for i in range(2)]
    whslab = [sb(f"whslab{i}", [128, DC, 128], F16) for i in range(2)]
    woslab = [sb(f"woslab{i}", [128, ECH, 128], F16) for i in range(2)]
    a_s = sb("a_s", [128, TT], F32)
    z_s = sb("z_s", [128, TT], F32)
    s_s = sb("s_s", [128, TT], F32)
    r_s = sb("r_s", [128, TT], F32)
    t_s = sb("t_s", [128, TT], F32)
    b_s = sb("b_s", [128, TT], F32)
    h_s = sb("h_s", [128, TT], F32)
    carry = sb("carry", [128, ECH], F32)
    xo = [sb(f"xo{i}", [128, D], F32) for i in range(2)]
    ln = LNHelper(c, "c", alloc=sb)

    xf_free = [0, 0]
    xo_free = [("Dso0", 0), ("Dso1", 0)]
    for st in range(NST):
        pst_free = 0
        tlast = 0
        for sub in range(NSUB):
            gi = st * NSUB + sub
            b = sub % 2
            dy = c.dma("y", xf[b][:], x_in[gi * 128:(gi + 1) * 128, :],
                       f"Dx{b}", waits=[("SV", xf_free[b])])
            lnw = ln.emit(xf[b][:], xnb[:], [dy],
                          extra_waits=[("ST", tlast)])
            xf_free[b] = lnw[1]
            for g4 in range(4):
                tv = None
                for i in range(g4 * 4, g4 * 4 + 4):
                    w = [lnw, ("SV", pst_free)] if i == g4 * 4 else ()
                    tv = c.op("t", lambda e, i=i, g4=g4: e.transpose(
                        pst[:, (i - g4 * 4) * 128:(i - g4 * 4 + 1) * 128],
                        xnb[:, i * 128:(i + 1) * 128], ident[:]),
                        inc="ST" if i == g4 * 4 + 3 else None, waits=w)
                tlast = tv
                pst_free = c.op("v", lambda e, g4=g4, sub=sub: e.tensor_copy(
                    xnTo[:, g4 * 4:(g4 + 1) * 4, sub * 128:(sub + 1) * 128],
                    pst[:].rearrange("p (c f) -> p c f", f=128)),
                    inc="SV", waits=[("ST", tv)])
        c.barrier()

        wg_free = [0, 0]
        wh_free = [0, 0]
        ps_free = [("SV", 0), ("SV", 0)]
        hb_cast = 0
        for ec in range(ECH):
            b = ec % 2
            dyg = c.dma("y", wgslab[b][:], whg_t4[ECH + ec], f"Dwg{b}",
                        waits=[("ST", wg_free[b])])
            dyh = c.dma("y", whslab[b][:], whg_t4[ec], f"Dwh{b}",
                        waits=[("ST", wh_free[b])])
            tvg = None
            for dc in range(DC):
                for nt in range(TT // 512):
                    w = ([dyg, ps_free[0]]
                         if (dc == 0 and nt == 0) else ())
                    tvg = c.op("t", lambda e, dc=dc, nt=nt, b=b: e.matmul(
                        psC[:, 0, nt * 512:(nt + 1) * 512], wgslab[b][:, dc, :],
                        xnTo[:, dc, nt * 512:(nt + 1) * 512],
                        start=(dc == 0), stop=(dc == DC - 1)),
                        inc="ST" if (dc == DC - 1 and nt == TT // 512 - 1) else None,
                        waits=w)
            wg_free[b] = tvg
            tvh = None
            for dc in range(DC):
                for nt in range(TT // 512):
                    w = ([dyh, ps_free[1]]
                         if (dc == 0 and nt == 0) else ())
                    tvh = c.op("t", lambda e, dc=dc, nt=nt, b=b: e.matmul(
                        psC[:, 1, nt * 512:(nt + 1) * 512], whslab[b][:, dc, :],
                        xnTo[:, dc, nt * 512:(nt + 1) * 512],
                        start=(dc == 0), stop=(dc == DC - 1)),
                        inc="ST" if (dc == DC - 1 and nt == TT // 512 - 1) else None,
                        waits=w)
            wh_free[b] = tvh
            sa = c.op("s", lambda e: e.activation(a_s[:], psC[:, 0, :], AF.Sigmoid,
                                                  scale=-1.0),
                      inc="SS", waits=[("ST", tvg), ("SV", hb_cast)])
            vz = c.op("v", lambda e: e.tensor_scalar(z_s[:], a_s[:], -1.0, 1.0,
                                                     op0=ALU.mult, op1=ALU.add),
                      inc="SV", waits=[("SS", sa)])
            ss = c.op("s", lambda e: e.activation(s_s[:], psC[:, 1, :], AF.Sigmoid),
                      inc="SS", waits=[("ST", tvh), ("SV", hb_cast)])
            vr = c.op("v", lambda e: e.tensor_scalar(r_s[:], psC[:, 1, :], 0.0, None,
                                                     op0=ALU.max),
                      inc="SV", waits=[("ST", tvh), ("SV", hb_cast), ("SS", ss)])
            ps_free[1] = ("SV", vr)
            ps_free[0] = ("SS", sa)
            vt = c.op("v", lambda e: e.scalar_tensor_tensor(
                t_s[:], s_s[:], 0.5, r_s[:], op0=ALU.min, op1=ALU.add),
                inc="SV", waits=[("SS", ss), ("SV", vr)])
            vb = c.op("v", lambda e: e.tensor_tensor(b_s[:], t_s[:], z_s[:],
                                                     op=ALU.mult),
                      inc="SV", waits=[("SV", vt), ("SV", vz)])
            init = 0.0 if st == 0 else carry[:, ec:ec + 1]
            vh = c.op("v", lambda e, init=init: e.tensor_tensor_scan(
                h_s[:], a_s[:], b_s[:], init, ALU.mult, ALU.add),
                inc="SV", waits=[("SV", vb)])
            vc = c.op("v", lambda e, ec=ec: e.tensor_copy(carry[:, ec:ec + 1],
                                                          h_s[:, TT - 1:TT]),
                      inc="SV", waits=[("SV", vh)])
            hb_cast = c.op("v", lambda e, ec=ec: e.tensor_copy(hb[:, ec, :], h_s[:]),
                           inc="SV", waits=[("SV", vc)])
        c.barrier()

        wo_free = [0, 0]
        ps_free = [0, 0]
        for dc in range(DC):
            b = dc % 2
            dy = c.dma("y", woslab[b][:], wout_t4[dc], f"Dwo{b}",
                       waits=[("ST", wo_free[b])])
            tv = None
            for ec in range(ECH):
                for nt in range(TT // 512):
                    w = ([dy, ("SV", ps_free[b])]
                         if (ec == 0 and nt == 0) else ())
                    tv = c.op("t", lambda e, ec=ec, nt=nt, b=b: e.matmul(
                        psC[:, b, nt * 512:(nt + 1) * 512], woslab[b][:, ec, :],
                        hb[:, ec, nt * 512:(nt + 1) * 512],
                        start=(ec == 0), stop=(ec == ECH - 1)),
                        inc="ST" if (ec == ECH - 1 and nt == TT // 512 - 1) else None,
                        waits=w)
            wo_free[b] = tv
            ps_free[b] = c.op("v", lambda e, b=b, dc=dc: e.tensor_copy(
                xnTo[:, dc, :], psC[:, b, :]),
                inc="SV", waits=[("ST", tv)])
        c.barrier()

        pstb_free = 0
        for tsub in range(NSUB):
            gi = st * NSUB + tsub
            b = tsub % 2
            dy = c.dma("y", xo[b][:], x_in[gi * 128:(gi + 1) * 128, :],
                       f"Dxo{b}", waits=[xo_free[b]])
            add_last = 0
            for g4 in range(4):
                tv = None
                for i in range(g4 * 4, g4 * 4 + 4):
                    dc = i
                    w = [("SV", pstb_free)] if i == g4 * 4 else ()
                    tv = c.op("t", lambda e, dc=dc, g4=g4, tsub=tsub: e.transpose(
                        pstb[:, (dc - g4 * 4) * 128:(dc - g4 * 4 + 1) * 128],
                        xnTo[:, dc, tsub * 128:(tsub + 1) * 128], ident[:]),
                        inc="ST" if i == g4 * 4 + 3 else None, waits=w)
                add_last = c.op("v", lambda e, g4=g4, b=b: e.tensor_tensor(
                    xo[b][:, g4 * 512:(g4 + 1) * 512], pstb[:],
                    xo[b][:, g4 * 512:(g4 + 1) * 512], op=ALU.add),
                    inc="SV", waits=[("ST", tv), dy])
                pstb_free = add_last
            xo_free[b] = c.dma("y", x_out[gi * 128:(gi + 1) * 128, :], xo[b][:],
                               f"Dso{b}", waits=[("SV", add_last)])
        c.barrier()
    es.close()



def build_phase_d(nc):
    c = Ctx(nc)
    x_in = nc.dram_tensor("x_in", [S, D], F32, kind="ExternalInput").ap()
    giT = nc.dram_tensor("giT", [D, 5504], I32, kind="ExternalInput").ap()
    uiT = nc.dram_tensor("uiT", [D, 5504], I32, kind="ExternalInput").ap()
    diT = nc.dram_tensor("diT", [5504, D], I32, kind="ExternalInput").ap()
    lut_in = nc.dram_tensor("lut", [1, 16], F32, kind="ExternalInput").ap()
    ident_in = nc.dram_tensor("ident", [128, 128], F16, kind="ExternalInput").ap()
    onesf_in = nc.dram_tensor("onesf", [128, 128], F32, kind="ExternalInput").ap()
    x_out = nc.dram_tensor("x_out", [S, D], F32, kind="ExternalOutput").ap()
    emit_phase_d(c, nc, x_in, x_out, giT, uiT, diT, lut_in, ident_in, onesf_in)
    c.finalize()
    return nc


def emit_phase_d(c, nc, x_in, x_out, giT, uiT, diT, lut_in, ident_in, onesf_in):
    OC5 = 43
    DC = 16
    INTER = 5504
    TT = 1024
    NST = S // TT
    NSUB = TT // 128
    es = ExitStack()
    def sb(name, shape, dtype):
        return es.enter_context(nc.sbuf_tensor("d_" + name, shape, dtype))
    def ps(name, shape, dtype=F32):
        return es.enter_context(nc.psum_tensor("d_" + name, shape, dtype))

    wg_t4 = c.dram("wg_t4", [OC5, 128, DC, 128], F16).ap()
    wu_t4 = c.dram("wu_t4", [OC5, 128, DC, 128], F16).ap()
    wd_t4 = c.dram("wd_t4", [DC, 128, OC5, 128], F16).ap()

    ident = sb("ident_s", [128, 128], F16)
    onesf = sb("onesf_s", [128, 128], F32)
    luts = sb("luts", [1, 16], F32)
    lutb = sb("lutb", [128, 16], F32)
    pst = ps("pst", [128, 512], F16)
    pstb = ps("pstb", [128, 512], F16)
    psA = ps("psA", [128, 2, TT], F32)
    psb3 = ps("psb3", [128, 16], F32)

    c.dma("y", ident[:], ident_in, "dDc")
    dy0 = c.dma("y", onesf[:], onesf_in, "dDc")
    dy1 = c.dma("y", luts[:], lut_in, "dDc")
    t0 = c.op("t", lambda e: e.matmul(psb3[:], onesf[0:1, :], luts[:], start=True,
                                      stop=True),
              inc="ST", waits=[dy1])
    lb = c.op("v", lambda e: e.tensor_copy(lutb[:], psb3[:]), inc="SV",
              waits=[("ST", t0)])

    prep_es = ExitStack()
    idxs = [prep_es.enter_context(nc.sbuf_tensor(f"d_idxs{i}", [128, INTER], I32))
            for i in range(2)]
    idxf = prep_es.enter_context(nc.sbuf_tensor("d_idxf", [128, INTER], F16))
    tmpb = prep_es.enter_context(nc.sbuf_tensor("d_tmpb", [128, INTER], F16))
    accb = prep_es.enter_context(nc.sbuf_tensor("d_accb", [128, INTER], F16))

    gw = [0, 0]
    vprev = 0
    wprev = ("Dgw", 0)
    chunk_list = ([("g", hc) for hc in range(DC)] + [("u", hc) for hc in range(DC)]
                  + [("d", oc) for oc in range(OC5)])
    for kind, ci in chunk_list:
        b = ci % 2
        cols = INTER if kind in ("g", "u") else D
        src = {"g": giT, "u": uiT, "d": diT}[kind]
        dl = c.dma("g", idxs[b][:, 0:cols], src[ci * 128:(ci + 1) * 128, :],
                   f"Dpi{b}", waits=[("SV", gw[b])])
        cv = c.op("v", lambda e, b=b, cols=cols: e.tensor_copy(
            idxf[:, 0:cols], idxs[b][:, 0:cols]),
            inc="SV", waits=[dl, ("SV", vprev)])
        gw[b] = cv
        hc_ = cols // 2
        rngs = [(0, hc_), (hc_, cols)]
        accv = [None, None]
        for v in range(16):
            tvs = [None, None]
            for hi, (a0, a1) in enumerate(rngs):
                if v == 0:
                    accv[hi] = c.op("v", lambda e, a0=a0, a1=a1, v=v:
                                    e.tensor_scalar(accb[:, a0:a1],
                                                    idxf[:, a0:a1], float(v),
                                                    lutb[:, v:v + 1],
                                                    op0=ALU.is_equal,
                                                    op1=ALU.mult),
                                    inc="SV",
                                    waits=[("SV", cv), ("SV", lb), wprev])
                else:
                    tvs[hi] = c.op("v", lambda e, a0=a0, a1=a1, v=v:
                                   e.tensor_scalar(tmpb[:, a0:a1],
                                                   idxf[:, a0:a1], float(v),
                                                   lutb[:, v:v + 1],
                                                   op0=ALU.is_equal,
                                                   op1=ALU.mult),
                                   inc="SV", waits=[("SV", accv[hi])])
            if v > 0:
                for hi, (a0, a1) in enumerate(rngs):
                    accv[hi] = c.op("v", lambda e, a0=a0, a1=a1: e.tensor_tensor(
                        accb[:, a0:a1], accb[:, a0:a1], tmpb[:, a0:a1],
                        op=ALU.add),
                        inc="SV", waits=[("SV", tvs[hi])])
        vprev = max(accv)
        if kind == "g":
            wprev = c.dma("g", wg_t4[:, :, ci, :].rearrange("o p f -> p o f"),
                          accb[:], "Dgw", waits=[("SV", vprev)])
        elif kind == "u":
            wprev = c.dma("g", wu_t4[:, :, ci, :].rearrange("o p f -> p o f"),
                          accb[:], "Dgw", waits=[("SV", vprev)])
        else:
            wprev = c.dma("g", wd_t4[:, :, ci, :].rearrange("d p f -> p d f"),
                          accb[:, 0:D], "Dgw", waits=[("SV", vprev)])
    c.barrier()
    prep_es.close()

    xf = [sb(f"xf{i}", [128, D], F32) for i in range(2)]
    xnb = sb("xnb", [128, D], F16)
    xnTo = sb("xnTo", [128, DC, TT], F16)
    hmb = sb("hmb", [128, OC5, TT], F16)
    wgslab = [sb(f"wgslab{i}", [128, DC, 128], F16) for i in range(2)]
    wuslab = [sb(f"wuslab{i}", [128, DC, 128], F16) for i in range(2)]
    wdslab = [sb(f"wdslab{i}", [128, OC5, 128], F16) for i in range(2)]
    sil = sb("sil", [128, TT], F32)
    xo = [sb(f"xo{i}", [128, D], F32) for i in range(2)]
    ln = LNHelper(c, "d", alloc=sb)

    xf_free = [0, 0]
    xo_free = [("Dso0", 0), ("Dso1", 0)]
    for st in range(NST):
        pst_free = 0
        tlast = 0
        for sub in range(NSUB):
            gi = st * NSUB + sub
            b = sub % 2
            dy = c.dma("y", xf[b][:], x_in[gi * 128:(gi + 1) * 128, :],
                       f"Dx{b}", waits=[("SV", xf_free[b])])
            lnw = ln.emit(xf[b][:], xnb[:], [dy],
                          extra_waits=[("ST", tlast)])
            xf_free[b] = lnw[1]
            for g4 in range(4):
                tv = None
                for i in range(g4 * 4, g4 * 4 + 4):
                    w = [lnw, ("SV", pst_free)] if i == g4 * 4 else ()
                    tv = c.op("t", lambda e, i=i, g4=g4: e.transpose(
                        pst[:, (i - g4 * 4) * 128:(i - g4 * 4 + 1) * 128],
                        xnb[:, i * 128:(i + 1) * 128], ident[:]),
                        inc="ST" if i == g4 * 4 + 3 else None, waits=w)
                tlast = tv
                pst_free = c.op("v", lambda e, g4=g4, sub=sub: e.tensor_copy(
                    xnTo[:, g4 * 4:(g4 + 1) * 4, sub * 128:(sub + 1) * 128],
                    pst[:].rearrange("p (c f) -> p c f", f=128)),
                    inc="SV", waits=[("ST", tv)])
        c.barrier()

        wg_free = [0, 0]
        wu_free = [0, 0]
        ps_free = [("SV", 0), ("SV", 0)]
        sil_free = 0
        for oc in range(OC5):
            b = oc % 2
            dyg = c.dma("y", wgslab[b][:], wg_t4[oc], f"Dwg{b}",
                        waits=[("ST", wg_free[b])])
            dyu = c.dma("y", wuslab[b][:], wu_t4[oc], f"Dwu{b}",
                        waits=[("ST", wu_free[b])])
            tvg = None
            for dc in range(DC):
                for nt in range(TT // 512):
                    w = ([dyg, ps_free[0]]
                         if (dc == 0 and nt == 0) else ())
                    tvg = c.op("t", lambda e, dc=dc, nt=nt, b=b: e.matmul(
                        psA[:, 0, nt * 512:(nt + 1) * 512], wgslab[b][:, dc, :],
                        xnTo[:, dc, nt * 512:(nt + 1) * 512],
                        start=(dc == 0), stop=(dc == DC - 1)),
                        inc="ST" if (dc == DC - 1 and nt == TT // 512 - 1) else None,
                        waits=w)
            wg_free[b] = tvg
            tvu = None
            for dc in range(DC):
                for nt in range(TT // 512):
                    w = ([dyu, ps_free[1]]
                         if (dc == 0 and nt == 0) else ())
                    tvu = c.op("t", lambda e, dc=dc, nt=nt, b=b: e.matmul(
                        psA[:, 1, nt * 512:(nt + 1) * 512], wuslab[b][:, dc, :],
                        xnTo[:, dc, nt * 512:(nt + 1) * 512],
                        start=(dc == 0), stop=(dc == DC - 1)),
                        inc="ST" if (dc == DC - 1 and nt == TT // 512 - 1) else None,
                        waits=w)
            wu_free[b] = tvu
            sv = c.op("s", lambda e: e.activation(sil[:], psA[:, 0, :], AF.Silu),
                      inc="SS", waits=[("ST", tvg), ("SV", sil_free)])
            ps_free[0] = ("SS", sv)
            hm = c.op("v", lambda e, oc=oc: e.tensor_tensor(
                hmb[:, oc, :], sil[:], psA[:, 1, :], op=ALU.mult),
                inc="SV", waits=[("SS", sv), ("ST", tvu)])
            ps_free[1] = ("SV", hm)
            sil_free = hm
        c.barrier()

        wd_free = [0, 0]
        ps_free = [0, 0]
        for dc in range(DC):
            b = dc % 2
            dy = c.dma("y", wdslab[b][:], wd_t4[dc], f"Dwd{b}",
                       waits=[("ST", wd_free[b])])
            tv = None
            for oc in range(OC5):
                for nt in range(TT // 512):
                    w = ([dy, ("SV", ps_free[b])]
                         if (oc == 0 and nt == 0) else ())
                    tv = c.op("t", lambda e, oc=oc, nt=nt, b=b: e.matmul(
                        psA[:, b, nt * 512:(nt + 1) * 512], wdslab[b][:, oc, :],
                        hmb[:, oc, nt * 512:(nt + 1) * 512],
                        start=(oc == 0), stop=(oc == OC5 - 1)),
                        inc="ST" if (oc == OC5 - 1 and nt == TT // 512 - 1) else None,
                        waits=w)
            wd_free[b] = tv
            ps_free[b] = c.op("v", lambda e, b=b, dc=dc: e.tensor_copy(
                xnTo[:, dc, :], psA[:, b, :]),
                inc="SV", waits=[("ST", tv)])
        c.barrier()

        pstb_free = 0
        for tsub in range(NSUB):
            gi = st * NSUB + tsub
            b = tsub % 2
            dy = c.dma("y", xo[b][:], x_in[gi * 128:(gi + 1) * 128, :],
                       f"Dxo{b}", waits=[xo_free[b]])
            add_last = 0
            for g4 in range(4):
                tv = None
                for i in range(g4 * 4, g4 * 4 + 4):
                    dc = i
                    w = [("SV", pstb_free)] if i == g4 * 4 else ()
                    tv = c.op("t", lambda e, dc=dc, g4=g4, tsub=tsub: e.transpose(
                        pstb[:, (dc - g4 * 4) * 128:(dc - g4 * 4 + 1) * 128],
                        xnTo[:, dc, tsub * 128:(tsub + 1) * 128], ident[:]),
                        inc="ST" if i == g4 * 4 + 3 else None, waits=w)
                add_last = c.op("v", lambda e, g4=g4, b=b: e.tensor_tensor(
                    xo[b][:, g4 * 512:(g4 + 1) * 512], pstb[:],
                    xo[b][:, g4 * 512:(g4 + 1) * 512], op=ALU.add),
                    inc="SV", waits=[("ST", tv), dy])
                pstb_free = add_last
            xo_free[b] = c.dma("y", x_out[gi * 128:(gi + 1) * 128, :], xo[b][:],
                               f"Dso{b}", waits=[("SV", add_last)])
        c.barrier()
    es.close()


def run_phase_c(x, w_hg, w_out):
    nc = bass.Bass("TRN2", target_bir_lowering=False)
    build_phase_c(nc)
    w_hg = np.ascontiguousarray(np.asarray(w_hg, dtype=np.float32))
    w_out = np.ascontiguousarray(np.asarray(w_out, dtype=np.float32))
    ident = _ident_f16()
    in_maps = [dict(x_in=np.ascontiguousarray(x[i]), w_hg=w_hg, w_out=w_out,
                    ident=ident) for i in range(NCORE)]
    res = run_bass_kernel_spmd(nc, in_maps, core_ids=list(range(NCORE)))
    return np.stack([r["x_out"] for r in res.results], axis=0)


def run_phase_d(x, lut, gate_idx, up_idx, down_idx):
    nc = bass.Bass("TRN2", target_bir_lowering=False)
    build_phase_d(nc)
    giT = np.ascontiguousarray(np.asarray(gate_idx).T.astype(np.int32))
    uiT = np.ascontiguousarray(np.asarray(up_idx).T.astype(np.int32))
    diT = np.ascontiguousarray(np.asarray(down_idx).T.astype(np.int32))
    lut2 = np.asarray(lut, dtype=np.float32).reshape(1, 16)
    ident = _ident_f16()
    onesf = _onesf()
    in_maps = [dict(x_in=np.ascontiguousarray(x[i]), giT=giT, uiT=uiT, diT=diT,
                    lut=lut2, ident=ident, onesf=onesf) for i in range(NCORE)]
    res = run_bass_kernel_spmd(nc, in_maps, core_ids=list(range(NCORE)))
    return np.stack([r["x_out"] for r in res.results], axis=0)



def _ident_bf16():
    import ml_dtypes
    return np.eye(128, dtype=ml_dtypes.bfloat16)


def _ident_f16():
    return np.eye(128, dtype=np.float16)


def _onesf():
    return np.ones((128, 128), dtype=np.float32)


def run_phase_b(x, w_fc, w_proj):
    nc = bass.Bass("TRN2", target_bir_lowering=False)
    build_phase_b(nc)
    wfcT = np.ascontiguousarray(np.asarray(w_fc).T)
    wprojT = np.ascontiguousarray(np.asarray(w_proj).T)
    ident = _ident_bf16()
    onesf = _onesf()
    in_maps = [dict(x_in=np.ascontiguousarray(x[i]), wfcT=wfcT, wprojT=wprojT,
                    ident=ident, onesf=onesf) for i in range(NCORE)]
    res = run_bass_kernel_spmd(nc, in_maps, core_ids=list(range(NCORE)))
    return np.stack([r["x_out"] for r in res.results], axis=0)


def build_fused(nc):
    c = Ctx(nc)
    x_in = nc.dram_tensor("x_in", [S, D], F32, kind="ExternalInput").ap()
    wq_in = nc.dram_tensor("wq", [D, D], F32, kind="ExternalInput").ap()
    wk_in = nc.dram_tensor("wk", [D, 512], F32, kind="ExternalInput").ap()
    wv_in = nc.dram_tensor("wv", [D, 512], F32, kind="ExternalInput").ap()
    wo_in = nc.dram_tensor("wo", [D, D], F32, kind="ExternalInput").ap()
    cos_in = nc.dram_tensor("cosq", [S, 512], F32, kind="ExternalInput").ap()
    sin_in = nc.dram_tensor("sinq", [S, 512], F32, kind="ExternalInput").ap()
    map_in = nc.dram_tensor("map16", [4, 16], F32, kind="ExternalInput").ap()
    identh_in = nc.dram_tensor("identh", [128, 128], F16, kind="ExternalInput").ap()
    identf_in = nc.dram_tensor("identf", [128, 128], F32, kind="ExternalInput").ap()
    onesf_in = nc.dram_tensor("onesf", [128, 128], F32, kind="ExternalInput").ap()
    identb_in = nc.dram_tensor("identb", [128, 128], BF16, kind="ExternalInput").ap()
    wfcT = nc.dram_tensor("wfcT", [D, 5120], F32, kind="ExternalInput").ap()
    wprojT = nc.dram_tensor("wprojT", [5120, D], F32, kind="ExternalInput").ap()
    whg_in = nc.dram_tensor("w_hg", [D, 6144], F32, kind="ExternalInput").ap()
    wout_in = nc.dram_tensor("w_out", [3072, D], F32, kind="ExternalInput").ap()
    giT = nc.dram_tensor("giT", [D, 5504], I32, kind="ExternalInput").ap()
    uiT = nc.dram_tensor("uiT", [D, 5504], I32, kind="ExternalInput").ap()
    diT = nc.dram_tensor("diT", [5504, D], I32, kind="ExternalInput").ap()
    lut_in = nc.dram_tensor("lut", [1, 16], F32, kind="ExternalInput").ap()
    x_out = nc.dram_tensor("x_out", [S, D], F32, kind="ExternalOutput").ap()

    x1 = c.dram("x1", [S, D], F32).ap()
    x2 = c.dram("x2", [S, D], F32).ap()
    x3 = c.dram("x3", [S, D], F32).ap()

    emit_phase_a(c, nc, x_in, x1, wq_in, wk_in, wv_in, wo_in, cos_in, sin_in,
                 map_in, identh_in, identf_in, onesf_in)
    emit_phase_b(c, nc, x1, x2, wfcT, wprojT, identb_in, onesf_in)
    emit_phase_c(c, nc, x2, x3, whg_in, wout_in, identh_in)
    emit_phase_d(c, nc, x3, x_out, giT, uiT, diT, lut_in, identh_in, onesf_in)
    c.finalize()
    return nc


def fused_in_maps(inputs):
    x = np.asarray(inputs["L0_attn_hidden_states"], dtype=np.float32)
    wq = np.ascontiguousarray(_perm_cols(inputs["L0_attn_Wq"], 16))
    wk = np.ascontiguousarray(_perm_cols(inputs["L0_attn_Wk"], 4))
    wv = np.ascontiguousarray(np.asarray(inputs["L0_attn_Wv"], dtype=np.float32))
    wo = np.ascontiguousarray(np.asarray(inputs["L0_attn_Wo"], dtype=np.float32))
    cosq, sinq = _rope_tables()
    map16 = np.zeros((4, 16), np.float32)
    for h in range(16):
        map16[h // 4, h] = 1.0
    wfcT = np.ascontiguousarray(np.asarray(inputs["L0_ffn_w_fc"]).T.astype(np.float32))
    wprojT = np.ascontiguousarray(np.asarray(inputs["L0_ffn_w_proj"]).T.astype(np.float32))
    w_hg = np.ascontiguousarray(np.asarray(inputs["L1_attn_w_hg"], dtype=np.float32))
    w_out = np.ascontiguousarray(np.asarray(inputs["L1_attn_w_out"], dtype=np.float32))
    giT = np.ascontiguousarray(np.asarray(inputs["L1_ffn_gate_idx"]).T.astype(np.int32))
    uiT = np.ascontiguousarray(np.asarray(inputs["L1_ffn_up_idx"]).T.astype(np.int32))
    diT = np.ascontiguousarray(np.asarray(inputs["L1_ffn_down_idx"]).T.astype(np.int32))
    lut2 = np.asarray(inputs["L1_ffn_lut"], dtype=np.float32).reshape(1, 16)
    shared = dict(wq=wq, wk=wk, wv=wv, wo=wo, cosq=cosq, sinq=sinq, map16=map16,
                  identh=_ident_f16(), identf=np.eye(128, dtype=np.float32),
                  onesf=_onesf(), identb=_ident_bf16(), wfcT=wfcT, wprojT=wprojT,
                  w_hg=w_hg, w_out=w_out, giT=giT, uiT=uiT, diT=diT, lut=lut2)
    return [dict(shared, x_in=np.ascontiguousarray(x[i])) for i in range(NCORE)]


def kernel(**inputs):
    nc = bass.Bass("TRN2", target_bir_lowering=False)
    build_fused(nc)
    in_maps = fused_in_maps(inputs)
    res = run_bass_kernel_spmd(nc, in_maps, core_ids=list(range(NCORE)))
    return np.stack([r["x_out"] for r in res.results], axis=0)


def kernel_unfused(**inputs):
    
    x = np.asarray(inputs["L0_attn_hidden_states"], dtype=np.float32)
    x = run_phase_a(x, np.asarray(inputs["L0_attn_Wq"]),
                    np.asarray(inputs["L0_attn_Wk"]),
                    np.asarray(inputs["L0_attn_Wv"]),
                    np.asarray(inputs["L0_attn_Wo"]))
    x = run_phase_b(x, np.asarray(inputs["L0_ffn_w_fc"]),
                    np.asarray(inputs["L0_ffn_w_proj"]))
    x = run_phase_c(x, np.asarray(inputs["L1_attn_w_hg"]),
                    np.asarray(inputs["L1_attn_w_out"]))
    x = run_phase_d(x, np.asarray(inputs["L1_ffn_lut"]),
                    np.asarray(inputs["L1_ffn_gate_idx"]),
                    np.asarray(inputs["L1_ffn_up_idx"]),
                    np.asarray(inputs["L1_ffn_down_idx"]))
    return x


# version 06 (final)
# speedup vs baseline: 1.0383x; 1.0383x over previous
import numpy as np
from contextlib import ExitStack

import concourse.bass as bass
import concourse.mybir as mybir
from concourse.bass_utils import run_bass_kernel_spmd

F32 = mybir.dt.float32
BF16 = mybir.dt.bfloat16
F16 = mybir.dt.float16
I32 = mybir.dt.int32
AF = mybir.ActivationFunctionType
ALU = mybir.AluOpType
AX = mybir.AxisListType

S = 4096
D = 2048
NCORE = 8

ENGINE_ATTRS = [("g", "gpsimd"), ("t", "tensor"), ("v", "vector"),
                ("s", "scalar"), ("y", "sync")]
BASE_SEMS = ["SV", "SS", "ST", "SG"]


class Ctx:

    def __init__(self, nc):
        self.nc = nc
        self.order = []
        self.sems = {name: 0 for name in BASE_SEMS}
        self.handles = {}
        self._es = ExitStack()

    def val(self, name):
        return self.sems[name]

    def sb(self, name, shape, dtype):
        return self.nc.alloc_sbuf_tensor(name, shape, dtype)

    def ps(self, name, shape, dtype=F32):
        return self._es.enter_context(self.nc.psum_tensor(name, shape, dtype))

    def dram(self, name, shape, dtype, kind="Internal"):
        return self.nc.dram_tensor(name, shape, dtype, kind=kind)

    def op(self, ek, build, inc=None, inc_n=1, waits=()):
        waits = [(s, v) for (s, v) in waits if v]
        for sn, _ in waits:
            self.sems.setdefault(sn, 0)
        target = None
        if inc is not None:
            self.sems.setdefault(inc, 0)
            self.sems[inc] += inc_n
            target = self.sems[inc]

        def closure(eng, build=build, waits=tuple(waits), inc=inc, inc_n=inc_n):
            for sname, v in waits:
                eng.wait_ge(self.handles[sname], v)
            inst = build(eng)
            if inc is not None:
                inst.then_inc(self.handles[inc], inc_n)

        self.order.append((ek, closure))
        return target

    def dma(self, ek, out, in_, inc, waits=()):
        v = self.op(ek, lambda e, out=out, in_=in_: e.dma_start(out=out, in_=in_),
                    inc=inc, inc_n=16, waits=waits)
        return (inc, v)

    def barrier(self):
        snap = tuple((s, v) for s, v in self.sems.items() if v)
        for ek, _ in ENGINE_ATTRS:
            def closure(eng, snap=snap):
                for sname, v in snap:
                    eng.wait_ge(self.handles[sname], v)
            self.order.append((ek, closure))

    def finalize(self, final_waits=None):
        fw = [(s, v) for s, v in self.sems.items() if v]

        def fin(eng, fw=tuple(fw)):
            for sname, v in fw:
                eng.wait_ge(self.handles[sname], v)
        self.order.append(("y", fin))
        for name in self.sems:
            self.handles[name] = self._es.enter_context(self.nc.semaphore(name))
        with self.nc.Block() as block:
            for key, attr in ENGINE_ATTRS:
                ops = [fn for k, fn in self.order if k == key]
                if ops:
                    getattr(block, attr)(
                        lambda eng, ops=tuple(ops): [f(eng) for f in ops])
        self._es.close()



class LNHelper:
    def __init__(self, ctx, tag, alloc=None):
        self.ctx = ctx
        alloc = alloc or ctx.sb
        self.sq = alloc(f"{tag}_lnsq", [128, D], BF16)
        self.st = alloc(f"{tag}_lnst", [128, 8], F32)
        self.rstd = alloc(f"{tag}_rstd", [128, 1], F32)
        self.negmr = alloc(f"{tag}_negmr", [128, 1], F32)
        self.prev = None

    def emit(self, xf_ap, xnb_ap, xwaits, extra_waits=()):
        c = self.ctx
        st = self.st
        w0 = list(xwaits) + list(extra_waits)
        if self.prev:
            w0 = w0 + [("SV", self.prev)]
        vss = c.op("s", lambda e, xf=xf_ap: e.activation(
            self.sq[:], xf, AF.Square, accum_out=st[:, 1:2]), inc="SS", waits=w0)
        vs0 = c.op("v", lambda e, xf=xf_ap: e.reduce_sum(st[:, 0:1], xf, axis=AX.X),
                   inc="SV", waits=w0)
        v1 = c.op("v", lambda e: e.tensor_scalar(st[:, 2:3], st[:, 0:1], 1.0 / D,
                                                 None, op0=ALU.mult),
                  inc="SV", waits=[("SV", vs0)])
        v2 = c.op("v", lambda e: e.tensor_scalar(st[:, 3:4], st[:, 1:2], 1.0 / D,
                                                 1e-5, op0=ALU.mult, op1=ALU.add),
                  inc="SV", waits=[("SS", vss)])
        v3 = c.op("v", lambda e: e.tensor_tensor(st[:, 4:5], st[:, 2:3], st[:, 2:3],
                                                 op=ALU.mult),
                  inc="SV", waits=[("SV", v1)])
        v4 = c.op("v", lambda e: e.scalar_tensor_tensor(st[:, 5:6], st[:, 4:5], -1.0,
                                                        st[:, 3:4], op0=ALU.mult,
                                                        op1=ALU.add),
                  inc="SV", waits=[("SV", v3), ("SV", v2)])
        s1 = c.op("s", lambda e: e.activation(st[:, 6:7], st[:, 5:6], AF.Ln),
                  inc="SS", waits=[("SV", v4)])
        s2 = c.op("s", lambda e: e.activation(self.rstd[:], st[:, 6:7], AF.Exp,
                                              scale=-0.5),
                  inc="SS", waits=[("SS", s1)])
        v5 = c.op("v", lambda e: e.tensor_scalar(self.negmr[:], st[:, 2:3],
                                                 self.rstd[:], -1.0,
                                                 op0=ALU.mult, op1=ALU.mult),
                  inc="SV", waits=[("SS", s2)])
        v6 = c.op("v", lambda e, xf=xf_ap, xn=xnb_ap: e.tensor_scalar(
            xn, xf, self.rstd[:], self.negmr[:], op0=ALU.mult, op1=ALU.add),
            inc="SV", waits=[("SV", v5)])
        self.prev = v6
        return ("SV", v6)



def build_phase_b(nc):
    c = Ctx(nc)
    x_in = nc.dram_tensor("x_in", [S, D], F32, kind="ExternalInput").ap()
    wfcT = nc.dram_tensor("wfcT", [D, 5120], F32, kind="ExternalInput").ap()
    wprojT = nc.dram_tensor("wprojT", [5120, D], F32, kind="ExternalInput").ap()
    ident_in = nc.dram_tensor("ident", [128, 128], BF16, kind="ExternalInput").ap()
    onesf_in = nc.dram_tensor("onesf", [128, 128], F32, kind="ExternalInput").ap()
    x_out = nc.dram_tensor("x_out", [S, D], F32, kind="ExternalOutput").ap()
    emit_phase_b(c, nc, x_in, x_out, wfcT, wprojT, ident_in, onesf_in)
    c.finalize()
    return nc


def emit_phase_b(c, nc, x_in, x_out, wfcT, wprojT, ident_in, onesf_in):
    OC, DC = 40, 16
    TT = 1024
    NST = S // TT
    NSUB = TT // 128
    es = ExitStack()
    def sb(name, shape, dtype):
        return es.enter_context(nc.sbuf_tensor("b_" + name, shape, dtype))
    def ps(name, shape, dtype=F32):
        return es.enter_context(nc.psum_tensor("b_" + name, shape, dtype))

    wfc_t4 = c.dram("wfc_t4", [OC, 128, DC, 128], BF16).ap()
    wproj_t4 = c.dram("wproj_t4", [DC, 128, OC, 128], BF16).ap()

    ident = sb("ident_s", [128, 128], BF16)
    onesf = sb("onesf_s", [128, 128], F32)
    row3 = sb("row3", [1, 4], F32)
    bc3 = sb("bc3", [128, 4], F32)
    sc1 = sb("sc1", [1, 8], F32)
    asum = sb("asum", [128, 64], F32)

    psA = ps("psA", [128, 2, TT], F32)
    pst = ps("pst", [128, 512], BF16)
    pstb = ps("pstb", [128, 512], BF16)
    psm = ps("psm", [1, 8], F32)
    psb3 = ps("psb3", [128, 4], F32)

    c.dma("y", ident[:], ident_in, "Dc")
    dy_const = c.dma("y", onesf[:], onesf_in, "Dc")

    prep_es = ExitStack()
    wbuf = [prep_es.enter_context(nc.sbuf_tensor(f"wbuf{i}", [128, 5120], F32))
            for i in range(2)]
    tern = prep_es.enter_context(nc.sbuf_tensor("tern", [128, 5120], BF16))
    sgn = prep_es.enter_context(nc.sbuf_tensor("sgn", [128, 5120], BF16))
    sqw = prep_es.enter_context(nc.sbuf_tensor("sqw", [128, 5120], F32))

    vlast = [0, 0]
    dgl = [0, 0]
    for pc in range(DC):
        b = pc % 2
        dgl[b] = c.dma("g", wbuf[b][:], wfcT[pc * 128:(pc + 1) * 128, :],
                       f"Dw{b}", waits=[("SV", vlast[b])])
        vlast[b] = c.op("v", lambda e, pc=pc, b=b: e.tensor_reduce(
            asum[:, pc:pc + 1], wbuf[b][:], axis=AX.X, op=ALU.add,
            apply_absolute_value=True), inc="SV", waits=[dgl[b]])
    for pc in range(OC):
        b = pc % 2
        dgl[b] = c.dma("g", wbuf[b][:, 0:2048], wprojT[pc * 128:(pc + 1) * 128, :],
                       f"Dw{b}", waits=[("SV", vlast[b])])
        vlast[b] = c.op("v", lambda e, pc=pc, b=b: e.tensor_reduce(
            asum[:, 16 + pc:17 + pc], wbuf[b][:, 0:2048], axis=AX.X, op=ALU.add,
            apply_absolute_value=True), inc="SV", waits=[dgl[b]])
    v1 = c.op("v", lambda e: e.tensor_reduce(asum[:, 56:57], asum[:, 0:16],
                                             axis=AX.X, op=ALU.add),
              inc="SV", waits=[("SV", vlast[0]), ("SV", vlast[1])])
    v2 = c.op("v", lambda e: e.tensor_reduce(asum[:, 57:58], asum[:, 16:56],
                                             axis=AX.X, op=ALU.add),
              inc="SV", waits=[("SV", v1)])
    t1 = c.op("t", lambda e: e.matmul(psm[:, 0:2], onesf[:, 0:1], asum[:, 56:58],
                                      start=True, stop=True),
              inc="ST", waits=[("SV", v2), dy_const])
    NFC = float(D * 5120)
    v3 = c.op("v", lambda e: e.tensor_scalar(sc1[:, 0:1], psm[:, 0:1], 1.0 / NFC,
                                             1e-5, op0=ALU.mult, op1=ALU.max),
              inc="SV", waits=[("ST", t1)])
    v4 = c.op("v", lambda e: e.tensor_scalar(sc1[:, 1:2], psm[:, 1:2], 1.0 / NFC,
                                             1e-5, op0=ALU.mult, op1=ALU.max),
              inc="SV", waits=[("SV", v3)])
    v5a = c.op("v", lambda e: e.tensor_scalar(sc1[:, 3:4], sc1[:, 0:1], 0.7, None,
                                              op0=ALU.mult),
               inc="SV", waits=[("SV", v4)])
    v5 = c.op("v", lambda e: e.tensor_tensor(row3[:, 0:1], sc1[:, 3:4], sc1[:, 3:4],
                                             op=ALU.mult),
              inc="SV", waits=[("SV", v5a)])
    v6a = c.op("v", lambda e: e.tensor_scalar(sc1[:, 4:5], sc1[:, 1:2], 0.7, None,
                                              op0=ALU.mult),
               inc="SV", waits=[("SV", v5)])
    v6 = c.op("v", lambda e: e.tensor_tensor(row3[:, 1:2], sc1[:, 4:5], sc1[:, 4:5],
                                             op=ALU.mult),
              inc="SV", waits=[("SV", v6a)])
    v7 = c.op("v", lambda e: e.tensor_tensor(sc1[:, 2:3], sc1[:, 0:1], sc1[:, 0:1],
                                             op=ALU.mult),
              inc="SV", waits=[("SV", v6)])
    v8 = c.op("v", lambda e: e.tensor_tensor(row3[:, 2:3], sc1[:, 2:3], sc1[:, 1:2],
                                             op=ALU.mult),
              inc="SV", waits=[("SV", v7)])
    vz3 = c.op("v", lambda e: e.memset(row3[:, 3:4], 0.0), inc="SV",
               waits=[("SV", v8)])
    t2 = c.op("t", lambda e: e.matmul(psb3[:], onesf[0:1, :], row3[:], start=True,
                                      stop=True),
              inc="ST", waits=[("SV", vz3)])
    c.op("v", lambda e: e.tensor_copy(bc3[:], psb3[:]), inc="SV",
         waits=[("ST", t2)])

    c.barrier()

    tn_prev = 0
    gw_prev = ("Dtw", 0)
    for pc in range(DC + OC):
        is_fc = pc < DC
        b = pc % 2
        cols = 5120 if is_fc else 2048
        src = (wfcT[pc * 128:(pc + 1) * 128, :] if is_fc
               else wprojT[(pc - DC) * 128:(pc - DC + 1) * 128, :])
        dgl[b] = c.dma("g", wbuf[b][:, 0:cols], src, f"Dw{b}",
                       waits=[("SV", vlast[b])])
        sg = c.op("s", lambda e, b=b, cols=cols: e.sign(sgn[:, 0:cols],
                                                        wbuf[b][:, 0:cols]),
                  inc="SS", waits=[dgl[b], ("SV", tn_prev)])
        thr = 0 if is_fc else 1
        mk = c.op("v", lambda e, b=b, cols=cols: e.tensor_tensor(
            sqw[:, 0:cols], wbuf[b][:, 0:cols], wbuf[b][:, 0:cols], op=ALU.mult),
            inc="SV", waits=[dgl[b], ("SV", tn_prev)])
        vlast[b] = mk
        tn = c.op("v", lambda e, cols=cols, thr=thr: e.scalar_tensor_tensor(
            tern[:, 0:cols], sqw[:, 0:cols], bc3[:, thr:thr + 1], sgn[:, 0:cols],
            op0=ALU.is_gt, op1=ALU.mult),
            inc="SV", waits=[("SS", sg), ("SV", mk), gw_prev])
        tn_prev = tn
        if is_fc:
            gw_prev = c.dma("g",
                            wfc_t4[:, :, pc, :].rearrange("o p f -> p o f"),
                            tern[:], "Dtw", waits=[("SV", tn)])
        else:
            gw_prev = c.dma("g",
                            wproj_t4[:, :, pc - DC, :].rearrange("d p f -> p d f"),
                            tern[:, 0:2048], "Dtw", waits=[("SV", tn)])

    c.barrier()
    prep_es.close()

    xf = [sb(f"xf{i}", [128, D], F32) for i in range(2)]
    xnb = sb("xnb", [128, D], BF16)
    xnTo = sb("xnTo", [128, DC, TT], BF16)
    wslab = [sb(f"wslab{i}", [128, DC, 128], BF16) for i in range(2)]
    wdslab = [sb(f"wdslab{i}", [128, OC, 128], BF16) for i in range(2)]
    relu_t = sb("relu_t", [128, TT], F32)
    hTb = sb("hTb", [128, OC, TT], BF16)
    xo = [sb(f"xo{i}", [128, D], F32) for i in range(2)]
    ln = LNHelper(c, "b", alloc=sb)

    xf_free = [0, 0]
    xo_free = [("Dso0", 0), ("Dso1", 0)]
    for st in range(NST):
        pst_free = 0
        tlast = 0
        for sub in range(NSUB):
            gi = st * NSUB + sub
            b = sub % 2
            dy = c.dma("y", xf[b][:], x_in[gi * 128:(gi + 1) * 128, :],
                       f"Dx{b}", waits=[("SV", xf_free[b])])
            lnw = ln.emit(xf[b][:], xnb[:], [dy],
                          extra_waits=[("ST", tlast)])
            xf_free[b] = lnw[1]
            for g4 in range(4):
                tv = None
                for i in range(g4 * 4, g4 * 4 + 4):
                    w = [lnw, ("SV", pst_free)] if i == g4 * 4 else ()
                    tv = c.op("t", lambda e, i=i, g4=g4: e.transpose(
                        pst[:, (i - g4 * 4) * 128:(i - g4 * 4 + 1) * 128],
                        xnb[:, i * 128:(i + 1) * 128], ident[:]),
                        inc="ST" if i == g4 * 4 + 3 else None, waits=w)
                tlast = tv
                pst_free = c.op("v", lambda e, g4=g4, sub=sub: e.tensor_copy(
                    xnTo[:, g4 * 4:(g4 + 1) * 4, sub * 128:(sub + 1) * 128],
                    pst[:].rearrange("p (c f) -> p c f", f=128)),
                    inc="SV", waits=[("ST", tv)])
        c.barrier()

        wslab_free = [0, 0]
        psA_free = [0, 0]
        sq_last = 0
        for oc in range(OC):
            b = oc % 2
            dy = c.dma("y", wslab[b][:], wfc_t4[oc], f"Dws{b}",
                       waits=[("ST", wslab_free[b])])
            tv = None
            for dc in range(DC):
                for nt in range(TT // 512):
                    w = ([dy, ("SV", psA_free[b])]
                         if (dc == 0 and nt == 0) else ())
                    tv = c.op("t", lambda e, dc=dc, nt=nt, b=b: e.matmul(
                        psA[:, b, nt * 512:(nt + 1) * 512], wslab[b][:, dc, :],
                        xnTo[:, dc, nt * 512:(nt + 1) * 512],
                        start=(dc == 0), stop=(dc == DC - 1)),
                        inc="ST" if (dc == DC - 1 and nt == TT // 512 - 1) else None,
                        waits=w)
            wslab_free[b] = tv
            rv = c.op("v", lambda e, b=b: e.tensor_scalar(
                relu_t[:], psA[:, b, :], 0.0, None, op0=ALU.max),
                inc="SV", waits=[("ST", tv), ("SS", sq_last)])
            sq_last = c.op("s", lambda e, oc=oc: e.activation(
                hTb[:, oc, :], relu_t[:], AF.Square),
                inc="SS", waits=[("SV", rv)])
            psA_free[b] = rv
        c.barrier()

        wdslab_free = [0, 0]
        psA_free = [0, 0]
        for dc in range(DC):
            b = dc % 2
            dy = c.dma("y", wdslab[b][:], wproj_t4[dc], f"Dwd{b}",
                       waits=[("ST", wdslab_free[b])])
            tv = None
            for oc in range(OC):
                for nt in range(TT // 512):
                    w = ([dy, ("SV", psA_free[b])]
                         if (oc == 0 and nt == 0) else ())
                    tv = c.op("t", lambda e, oc=oc, nt=nt, b=b: e.matmul(
                        psA[:, b, nt * 512:(nt + 1) * 512], wdslab[b][:, oc, :],
                        hTb[:, oc, nt * 512:(nt + 1) * 512],
                        start=(oc == 0), stop=(oc == OC - 1)),
                        inc="ST" if (oc == OC - 1 and nt == TT // 512 - 1) else None,
                        waits=w)
            wdslab_free[b] = tv
            psA_free[b] = c.op("v", lambda e, b=b, dc=dc: e.tensor_scalar(
                xnTo[:, dc, :], psA[:, b, :], bc3[:, 2:3], None, op0=ALU.mult),
                inc="SV", waits=[("ST", tv)])
        c.barrier()

        pstb_free = 0
        for tsub in range(NSUB):
            gi = st * NSUB + tsub
            b = tsub % 2
            dy = c.dma("y", xo[b][:], x_in[gi * 128:(gi + 1) * 128, :],
                       f"Dxo{b}", waits=[xo_free[b]])
            add_last = 0
            for g4 in range(4):
                tv = None
                for i in range(g4 * 4, g4 * 4 + 4):
                    dc = i
                    w = [("SV", pstb_free)] if i == g4 * 4 else ()
                    tv = c.op("t", lambda e, dc=dc, g4=g4, tsub=tsub: e.transpose(
                        pstb[:, (dc - g4 * 4) * 128:(dc - g4 * 4 + 1) * 128],
                        xnTo[:, dc, tsub * 128:(tsub + 1) * 128], ident[:]),
                        inc="ST" if i == g4 * 4 + 3 else None, waits=w)
                add_last = c.op("v", lambda e, g4=g4, b=b: e.tensor_tensor(
                    xo[b][:, g4 * 512:(g4 + 1) * 512], pstb[:],
                    xo[b][:, g4 * 512:(g4 + 1) * 512], op=ALU.add),
                    inc="SV", waits=[("ST", tv), dy])
                pstb_free = add_last
            xo_free[b] = c.dma("y", x_out[gi * 128:(gi + 1) * 128, :], xo[b][:],
                               f"Dso{b}", waits=[("SV", add_last)])
        c.barrier()
    es.close()





def _emit_attn_epilogue(c, a2s, psR, psO, onesf, drecip, rb, aoT):
    h, rc, pv_st = a2s["pend"]
    tb = c.op("t", lambda e: e.matmul(psR[:], onesf[0:1, :], drecip[:],
                                      start=True, stop=True),
              inc="ST", waits=[("SV", rc), ("SV", a2s["psO_free"])])
    rbc = c.op("v", lambda e: e.tensor_copy(rb[:], psR[:]),
               inc="SV", waits=[("ST", tb)])
    a2s["rb_free"] = rbc
    ao = c.op("v", lambda e, h=h: e.tensor_tensor(
        aoT[:, h, :], psO[:], rb[:], op=ALU.mult),
        inc="SV", waits=[("SV", rbc), ("ST", pv_st)])
    a2s["psO_free"] = ao
    a2s["ao_done"] = ao
    a2s["pend"] = None


def build_phase_a(nc, debug=0):
    c = Ctx(nc)
    x_in = nc.dram_tensor("x_in", [S, D], F32, kind="ExternalInput").ap()
    wq_in = nc.dram_tensor("wq", [D, D], F32, kind="ExternalInput").ap()
    wk_in = nc.dram_tensor("wk", [D, 512], F32, kind="ExternalInput").ap()
    wv_in = nc.dram_tensor("wv", [D, 512], F32, kind="ExternalInput").ap()
    wo_in = nc.dram_tensor("wo", [D, D], F32, kind="ExternalInput").ap()
    cos_in = nc.dram_tensor("cosq", [S, 512], F32, kind="ExternalInput").ap()
    sin_in = nc.dram_tensor("sinq", [S, 512], F32, kind="ExternalInput").ap()
    map_in = nc.dram_tensor("map16", [4, 16], F32, kind="ExternalInput").ap()
    ident_in = nc.dram_tensor("identh", [128, 128], F16, kind="ExternalInput").ap()
    identf_in = nc.dram_tensor("identf", [128, 128], F32, kind="ExternalInput").ap()
    onesf_in = nc.dram_tensor("onesf", [128, 128], F32, kind="ExternalInput").ap()
    x_out = nc.dram_tensor("x_out", [S, D], F32, kind="ExternalOutput").ap()
    emit_phase_a(c, nc, x_in, x_out, wq_in, wk_in, wv_in, wo_in, cos_in, sin_in,
                 map_in, ident_in, identf_in, onesf_in, debug=debug)
    c.finalize()
    return nc


def emit_phase_a(c, nc, x_in, x_out, wq_in, wk_in, wv_in, wo_in, cos_in, sin_in,
                 map_in, ident_in, identf_in, onesf_in, debug=0):
    DC = 16
    NH, NKV = 16, 4
    NT = 32
    QT = 512
    NQT = S // QT
    ISR = 0.08838834764831845
    pa_es = ExitStack()
    def sb(name, shape, dtype):
        return pa_es.enter_context(nc.sbuf_tensor("a_" + name, shape, dtype))
    def ps(name, shape, dtype=F32):
        return pa_es.enter_context(nc.psum_tensor("a_" + name, shape, dtype))

    xnb_h = c.dram("xnb_h", [S, D], F16).ap()
    qb_h = c.dram("qb_h", [S, D], F16).ap()

    identh = sb("identh_s", [128, 128], F16)
    identf = sb("identf_s", [128, 128], F32)
    onesf = sb("onesf_s", [128, 128], F32)
    onesh = sb("onesh_s", [128, 1], F16)
    map16 = sb("map16_s", [4, 16], F32)
    kT = sb("kT", [128, NKV, S], F16)
    vst = sb("vst", [128, NT, 512], F16)
    kn2m = sb("kn2m", [128, 4], F32)
    qn2m = sb("qn2m", [128, 16], F32)
    negB = sb("negB", [128, 16], F32)
    sc16 = sb("sc16", [16, 8], F32)
    row16 = sb("row16", [1, 16], F32)

    pst = ps("pst", [128, 512], F16)
    psB = ps("psB", [128, 4, 512], F32)
    psO = ps("psO", [128, 512], F32)
    psDen = ps("psDen", [1, 512], F32)
    psR = ps("psR", [128, 512], F32)

    c.dma("y", identh[:], ident_in, "Dc")
    c.dma("y", identf[:], identf_in, "Dc")
    c.dma("y", onesf[:], onesf_in, "Dc")
    dyc = c.dma("y", map16[:], map_in, "Dc")
    ms = c.op("g", lambda e: e.memset(onesh[:], 1.0), inc="SG")
    z1 = c.op("v", lambda e: e.memset(kn2m[:], 0.0), inc="SV")
    z2 = c.op("v", lambda e: e.memset(qn2m[:], 0.0), inc="SV")
    consts_w = [dyc, ("SG", ms)]

    a1_es = ExitStack()
    wkv = a1_es.enter_context(nc.sbuf_tensor("wkv", [128, DC, 1024], F16))
    xf = [a1_es.enter_context(nc.sbuf_tensor(f"xf{i}", [128, D], F32))
          for i in range(2)]
    xnb = a1_es.enter_context(nc.sbuf_tensor("xnb", [128, D], F16))
    xnT = a1_es.enter_context(nc.sbuf_tensor("xnT", [128, DC, 128], F16))
    ctile = [a1_es.enter_context(nc.sbuf_tensor(f"ctile{i}", [128, 512], F32))
             for i in range(2)]
    stile = [a1_es.enter_context(nc.sbuf_tensor(f"stile{i}", [128, 512], F32))
             for i in range(2)]
    rs = a1_es.enter_context(nc.sbuf_tensor("rs", [128, 4, 128], F32))
    kb = a1_es.enter_context(nc.sbuf_tensor("kb", [128, 512], F16))
    ksq = a1_es.enter_context(nc.sbuf_tensor("ksq", [128, 512], F16))
    kst = a1_es.enter_context(nc.sbuf_tensor("kst", [128, 4], F32))
    ln = LNHelper(c, "a", alloc=lambda n, sh, dt: a1_es.enter_context(
        nc.sbuf_tensor(n, sh, dt)))

    dk = c.dma("g", wkv[:, :, 0:512],
               wk_in.rearrange("(c p) f -> p c f", p=128), "Dwk")
    dv = c.dma("g", wkv[:, :, 512:1024],
               wv_in.rearrange("(c p) f -> p c f", p=128), "Dwk")

    xf_free = [0, 0]
    ct_free = [0, 0]
    pst_free = 0
    tlast = 0
    kv_done = 0
    knm_prev = 0
    rope_read = 0
    vcopy_prev = 0
    xnb_st = ("Dxn", 0)
    NT_A1A = 3 if (debug == 5 or debug >= 10) else NT
    for ti in range(NT_A1A):
        b = ti % 2
        dy = c.dma("y", xf[b][:], x_in[ti * 128:(ti + 1) * 128, :],
                   f"Dx{b}", waits=[("SV", xf_free[b])])
        dct = c.dma("y", ctile[b][:, 0:128], cos_in[ti * 128:(ti + 1) * 128, 0:128],
                    f"Dct{b}", waits=[("SV", ct_free[b])])
        dst_ = c.dma("y", stile[b][:, 0:128], sin_in[ti * 128:(ti + 1) * 128, 0:128],
                     f"Dsn{b}", waits=[("SV", ct_free[b])])
        lnw = ln.emit(xf[b][:], xnb[:], [dy],
                      extra_waits=[("ST", tlast), xnb_st])
        xf_free[b] = lnw[1]
        xnb_st = c.dma("y", xnb_h[ti * 128:(ti + 1) * 128, :], xnb[:], "Dxn",
                       waits=[lnw])
        if 10 <= debug < 11:
            continue
        for g4 in range(4):
            tv = None
            for i in range(g4 * 4, g4 * 4 + 4):
                w = [lnw, ("SV", pst_free), ("ST", kv_done)] if i == g4 * 4 else ()
                if ti == 0 and g4 == 0 and i == 0:
                    w = list(w) + [("Dc", c.val("Dc"))]
                tv = c.op("t", lambda e, i=i, g4=g4: e.transpose(
                    pst[:, (i - g4 * 4) * 128:(i - g4 * 4 + 1) * 128],
                    xnb[:, i * 128:(i + 1) * 128], identh[:]),
                    inc="ST" if i == g4 * 4 + 3 else None, waits=w)
            tlast = tv
            pst_free = c.op("v", lambda e, g4=g4: e.tensor_copy(
                xnT[:, g4 * 4:(g4 + 1) * 4, :],
                pst[:].rearrange("p (c f) -> p c f", f=128)),
                inc="SV", waits=[("ST", tv)])
        if 10 <= debug < 12:
            continue
        tv = None
        for dc in range(DC):
            for n in range(2):
                w = ([("SV", pst_free), dv, ("SV", rope_read),
                      ("SS", vcopy_prev)] if (dc == 0 and n == 0) else ())
                tv = c.op("t", lambda e, dc=dc, n=n: e.matmul(
                    psB[:, n, :], xnT[:, dc, :], wkv[:, dc, n * 512:(n + 1) * 512],
                    start=(dc == 0), stop=(dc == DC - 1)),
                    inc="ST" if (dc == DC - 1 and n == 1) else None, waits=w)
        kv_done = tv
        if 10 <= debug < 13:
            continue
        kv_v = psB[:, 0, :].rearrange("p (g f) -> p g f", f=128)
        cv = ctile[b][:, 0:128].rearrange("p (g f) -> p g f", f=32)
        sv_ = stile[b][:, 0:128].rearrange("p (g f) -> p g f", f=32)
        kbv = kb[:].rearrange("p (g f) -> p g f", f=128)
        r1 = c.op("v", lambda e, kv_v=kv_v, cv=cv: e.tensor_tensor(
            rs[:, :, 0:32], kv_v[:, :, 0:32], cv, op=ALU.mult),
            inc="SV", waits=[("ST", kv_done), dct, dst_, ("SV", knm_prev)])
        r2 = c.op("v", lambda e, kv_v=kv_v, sv_=sv_: e.tensor_tensor(
            rs[:, :, 32:64], kv_v[:, :, 32:64], sv_, op=ALU.mult),
            inc="SV", waits=())
        if debug == 131:
            rope_read = r2
            ct_free[b] = r2
            continue
        r3 = c.op("v", lambda e, kv_v=kv_v, sv_=sv_: e.tensor_tensor(
            rs[:, :, 64:96], kv_v[:, :, 0:32], sv_, op=ALU.mult),
            inc="SV", waits=())
        r4 = c.op("v", lambda e, kv_v=kv_v, cv=cv: e.tensor_tensor(
            rs[:, :, 96:128], kv_v[:, :, 32:64], cv, op=ALU.mult),
            inc="SV", waits=())
        if debug == 132:
            rope_read = r4
            ct_free[b] = r4
            continue
        r5 = c.op("v", lambda e, kbv=kbv: e.tensor_sub(
            kbv[:, :, 0:32], rs[:, :, 0:32], rs[:, :, 32:64]),
            inc="SV", waits=[("SV", r2)])
        r6 = c.op("v", lambda e, kbv=kbv: e.tensor_add(
            kbv[:, :, 32:64], rs[:, :, 64:96], rs[:, :, 96:128]),
            inc="SV", waits=[("SV", r4)])
        if debug == 133:
            rope_read = r6
            ct_free[b] = r6
            continue
        r7 = c.op("v", lambda e, kbv=kbv, kv_v=kv_v: e.tensor_copy(
            kbv[:, :, 64:128], kv_v[:, :, 64:128]),
            inc="SV", waits=[("ST", kv_done), ("SV", knm_prev)])
        ct_free[b] = r6
        rope_read = r6
        if 10 <= debug < 14 or debug in (131, 132, 133):
            continue
        vcopy_prev = c.op("s", lambda e, ti=ti: e.copy(vst[:, ti, :], psB[:, 1, :]),
                          inc="SS", waits=[("SV", r7)])
        n1 = c.op("v", lambda e: e.tensor_mul(ksq[:], kb[:], kb[:]),
                  inc="SV", waits=[("SV", r5), ("SV", r6), ("SV", r7)])
        n2 = c.op("v", lambda e: e.tensor_reduce(
            kst[:], ksq[:].rearrange("p (g f) -> p g f", f=128), axis=AX.X,
            op=ALU.add), inc="SV", waits=[("SV", n1)])
        knm_prev = c.op("v", lambda e: e.tensor_max(kn2m[:], kn2m[:], kst[:]),
                        inc="SV", waits=[("SV", n2), ("SV", z1)])
        if 10 <= debug < 15:
            continue
        tv = None
        for i in range(4):
            w = ([("SV", r5), ("SV", r6), ("SV", r7), ("SV", pst_free)]
                 if i == 0 else ())
            tv = c.op("t", lambda e, i=i: e.transpose(
                pst[:, i * 128:(i + 1) * 128], kb[:, i * 128:(i + 1) * 128],
                identh[:]), inc="ST" if i == 3 else None, waits=w)
        pst_free = c.op("v", lambda e, ti=ti: e.tensor_copy(
            kT[:, :, ti * 128:(ti + 1) * 128],
            pst[:].rearrange("p (g f) -> p g f", f=128)),
            inc="SV", waits=[("ST", tv)])
        tlast = tv
    c.barrier()
    a1_es.close()

    if debug in (2, 5) or debug >= 10:
        pa_es.close()
        return

    a1b_es = ExitStack()
    wqs = a1b_es.enter_context(nc.sbuf_tensor("wqs", [128, DC, 2048], F16))
    xnbt = [a1b_es.enter_context(nc.sbuf_tensor(f"xnbt{i}", [128, D], F16))
            for i in range(2)]
    xnTb = a1b_es.enter_context(nc.sbuf_tensor("xnTb", [128, DC, 128], F16))
    ctile = [a1b_es.enter_context(nc.sbuf_tensor(f"ctileb{i}", [128, 512], F32))
             for i in range(2)]
    stile = [a1b_es.enter_context(nc.sbuf_tensor(f"stileb{i}", [128, 512], F32))
             for i in range(2)]
    rsb = a1b_es.enter_context(nc.sbuf_tensor("rsb", [128, 16, 128], F32))
    qb = a1b_es.enter_context(nc.sbuf_tensor("qb", [128, D], F16))
    qsq = a1b_es.enter_context(nc.sbuf_tensor("qsq", [128, D], F16))
    qst = a1b_es.enter_context(nc.sbuf_tensor("qst", [128, 16], F32))

    dq = c.dma("g", wqs[:], wq_in.rearrange("(c p) f -> p c f", p=128), "Dwq")
    xn_free = [0, 0]
    ct_free = [0, 0]
    pst_free = 0
    q_done = 0
    qnm_prev = 0
    rope_read = 0
    qb_store = ("Dqb", 0)
    for ti in range(NT):
        b = ti % 2
        dxn = c.dma("y", xnbt[b][:], xnb_h[ti * 128:(ti + 1) * 128, :],
                    f"Dxq{b}", waits=[("ST", xn_free[b])])
        dct = c.dma("y", ctile[b][:], cos_in[ti * 128:(ti + 1) * 128, :],
                    f"Dct{b}", waits=[("SV", ct_free[b])])
        dst_ = c.dma("y", stile[b][:], sin_in[ti * 128:(ti + 1) * 128, :],
                     f"Dsn{b}", waits=[("SV", ct_free[b])])
        for g4 in range(4):
            tv = None
            for i in range(g4 * 4, g4 * 4 + 4):
                w = ([dxn, ("SV", pst_free), ("ST", q_done)]
                     if i == g4 * 4 else ())
                tv = c.op("t", lambda e, i=i, g4=g4, b=b: e.transpose(
                    pst[:, (i - g4 * 4) * 128:(i - g4 * 4 + 1) * 128],
                    xnbt[b][:, i * 128:(i + 1) * 128], identh[:]),
                    inc="ST" if i == g4 * 4 + 3 else None, waits=w)
            xn_free[b] = tv
            pst_free = c.op("v", lambda e, g4=g4: e.tensor_copy(
                xnTb[:, g4 * 4:(g4 + 1) * 4, :],
                pst[:].rearrange("p (c f) -> p c f", f=128)),
                inc="SV", waits=[("ST", tv)])
        tv = None
        for dc in range(DC):
            for n in range(4):
                w = ([("SV", pst_free), dq, ("SV", rope_read)]
                     if (dc == 0 and n == 0) else ())
                tv = c.op("t", lambda e, dc=dc, n=n: e.matmul(
                    psB[:, n, :], xnTb[:, dc, :], wqs[:, dc, n * 512:(n + 1) * 512],
                    start=(dc == 0), stop=(dc == DC - 1)),
                    inc="ST" if (dc == DC - 1 and n == 3) else None, waits=w)
        q_done = tv
        qv = psB[:].rearrange("p n (h f) -> p (n h) f", f=128)
        cv = ctile[b][:].rearrange("p (h f) -> p h f", f=32)
        sv_ = stile[b][:].rearrange("p (h f) -> p h f", f=32)
        qbv = qb[:].rearrange("p (h f) -> p h f", f=128)
        r1 = c.op("v", lambda e, qv=qv, cv=cv: e.tensor_tensor(
            rsb[:, :, 0:32], qv[:, :, 0:32], cv, op=ALU.mult),
            inc="SV", waits=[("ST", q_done), dct, dst_, ("SV", qnm_prev),
                             qb_store])
        r2 = c.op("v", lambda e, qv=qv, sv_=sv_: e.tensor_tensor(
            rsb[:, :, 32:64], qv[:, :, 32:64], sv_, op=ALU.mult),
            inc="SV", waits=())
        r3 = c.op("v", lambda e, qv=qv, sv_=sv_: e.tensor_tensor(
            rsb[:, :, 64:96], qv[:, :, 0:32], sv_, op=ALU.mult),
            inc="SV", waits=())
        r4 = c.op("v", lambda e, qv=qv, cv=cv: e.tensor_tensor(
            rsb[:, :, 96:128], qv[:, :, 32:64], cv, op=ALU.mult),
            inc="SV", waits=())
        r5 = c.op("v", lambda e, qbv=qbv: e.tensor_sub(
            qbv[:, :, 0:32], rsb[:, :, 0:32], rsb[:, :, 32:64]),
            inc="SV", waits=[("SV", r2)])
        r6 = c.op("v", lambda e, qbv=qbv: e.tensor_add(
            qbv[:, :, 32:64], rsb[:, :, 64:96], rsb[:, :, 96:128]),
            inc="SV", waits=[("SV", r4)])
        r7 = c.op("v", lambda e, qbv=qbv, qv=qv: e.tensor_copy(
            qbv[:, :, 64:128], qv[:, :, 64:128]),
            inc="SV", waits=[("ST", q_done), qb_store, ("SV", qnm_prev)])
        ct_free[b] = r6
        rope_read = r6
        n1 = c.op("v", lambda e: e.tensor_mul(qsq[:], qb[:], qb[:]),
                  inc="SV", waits=[("SV", r5), ("SV", r6), ("SV", r7)])
        n2 = c.op("v", lambda e: e.tensor_reduce(
            qst[:], qsq[:].rearrange("p (h f) -> p h f", f=128), axis=AX.X,
            op=ALU.add), inc="SV", waits=[("SV", n1)])
        qnm_prev = c.op("v", lambda e: e.tensor_max(qn2m[:], qn2m[:], qst[:]),
                        inc="SV", waits=[("SV", n2), ("SV", z2)])
        qb_store = c.dma("y", qb_h[ti * 128:(ti + 1) * 128, :], qb[:], "Dqb",
                         waits=[("SV", qnm_prev)])
    c.barrier()
    a1b_es.close()

    if debug == 3:
        pa_es.close()
        return

    t1 = c.op("t", lambda e: e.transpose(psR[0:16, 0:128], qn2m[:], identf[:]),
              inc="ST", waits=consts_w)
    v1 = c.op("v", lambda e: e.reduce_max(sc16[:, 0:1], psR[0:16, 0:128],
                                          axis=AX.X),
              inc="SV", waits=[("ST", t1)])
    t2 = c.op("t", lambda e: e.transpose(psR[0:4, 0:128], kn2m[:], identf[:]),
              inc="ST", waits=[("SV", v1)])
    v2 = c.op("v", lambda e: e.reduce_max(sc16[0:4, 1:2], psR[0:4, 0:128],
                                          axis=AX.X),
              inc="SV", waits=[("ST", t2)])
    t3 = c.op("t", lambda e: e.matmul(psR[0:16, 0:1], map16[:], sc16[0:4, 1:2],
                                      start=True, stop=True),
              inc="ST", waits=[("SV", v2)])
    v3 = c.op("v", lambda e: e.tensor_copy(sc16[:, 2:3], psR[0:16, 0:1]),
              inc="SV", waits=[("ST", t3)])
    v4 = c.op("v", lambda e: e.tensor_tensor(sc16[:, 3:4], sc16[:, 0:1],
                                             sc16[:, 2:3], op=ALU.mult),
              inc="SV", waits=[("SV", v3)])
    s1 = c.op("s", lambda e: e.activation(sc16[:, 4:5], sc16[:, 3:4], AF.Ln),
              inc="SS", waits=[("SV", v4)])
    s2 = c.op("s", lambda e: e.activation(sc16[:, 5:6], sc16[:, 4:5], AF.Exp,
                                          scale=0.5),
              inc="SS", waits=[("SS", s1)])
    v5 = c.op("v", lambda e: e.tensor_scalar(sc16[:, 6:7], sc16[:, 5:6], -ISR,
                                             None, op0=ALU.mult),
              inc="SV", waits=[("SS", s2)])
    t4 = c.op("t", lambda e: e.transpose(psR[0:1, 0:16], sc16[:, 6:7], identf[0:16, 0:16]),
              inc="ST", waits=[("SV", v5)])
    v6 = c.op("v", lambda e: e.tensor_copy(row16[:], psR[0:1, 0:16]),
              inc="SV", waits=[("ST", t4)])
    t5 = c.op("t", lambda e: e.matmul(psR[:, 0:16], onesf[0:1, :], row16[:],
                                      start=True, stop=True),
              inc="ST", waits=[("SV", v6)])
    v7 = c.op("v", lambda e: e.tensor_copy(negB[:], psR[:, 0:16]),
              inc="SV", waits=[("ST", t5)])
    negb_w = ("SV", v7)
    c.barrier()

    if debug == 1:
        pa_es.close()
        return

    a2_es = ExitStack()
    wos = a2_es.enter_context(nc.sbuf_tensor("wos", [128, DC, 2048], F16))
    qT = a2_es.enter_context(nc.sbuf_tensor("qT", [128, NH, QT], F16))
    qbl = [a2_es.enter_context(nc.sbuf_tensor(f"qbl{i}", [128, D], F16))
           for i in range(2)]
    Pb = a2_es.enter_context(nc.sbuf_tensor("Pb", [128, 4, 1024], F16))
    u2 = a2_es.enter_context(nc.sbuf_tensor("u2", [128, 2, 512], F16))
    dsum = a2_es.enter_context(nc.sbuf_tensor("dsum", [128, 8, 512], F16))
    densb = a2_es.enter_context(nc.sbuf_tensor("densb", [1, 512], F32))
    drecip = a2_es.enter_context(nc.sbuf_tensor("drecip", [1, 512], F32))
    dscr = a2_es.enter_context(nc.sbuf_tensor("dscr", [1, 512], F32))
    rb = a2_es.enter_context(nc.sbuf_tensor("rb", [128, 512], F32))
    aoT = a2_es.enter_context(nc.sbuf_tensor("aoT", [128, NH, QT], F16))
    xr0 = a2_es.enter_context(nc.sbuf_tensor("xr0", [128, D], F32))
    xr = [xr0, xr0]

    dwo = c.dma("g", wos[:], wo_in.rearrange("(c p) f -> p c f", p=128), "Dwo")

    xr_free = [("Dso0", 0), ("Dso0", 0)]
    qbl_free = [0, 0]
    pst_free = 0
    wo_done = 0
    add4 = 0
    a2s = dict(pend=None, exp_free=[(0, 0)] * 4, prev_exp=[0, 0],
               psO_free=0, den_free=0, rb_free=0, u_free=0, dsum_free=0,
               ao_done=0, add4_prev=0)
    for qt in range(NQT):
        qT_ready = 0
        for sub in range(4):
            b = sub % 2
            gi = qt * 4 + sub
            dq2 = c.dma("y", qbl[b][:], qb_h[gi * 128:(gi + 1) * 128, :],
                        f"Dq2{b}", waits=[("ST", qbl_free[b])])
            for g4 in range(4):
                tv = None
                for i in range(g4 * 4, g4 * 4 + 4):
                    w = ([dq2, ("SV", pst_free), ("ST", wo_done)]
                         if i == g4 * 4 else ())
                    tv = c.op("t", lambda e, i=i, g4=g4, b=b: e.transpose(
                        pst[:, (i - g4 * 4) * 128:(i - g4 * 4 + 1) * 128],
                        qbl[b][:, i * 128:(i + 1) * 128], identh[:]),
                        inc="ST" if i == g4 * 4 + 3 else None, waits=w)
                qbl_free[b] = tv
                pst_free = c.op("v", lambda e, g4=g4, sub=sub: e.tensor_copy(
                    qT[:, g4 * 4:(g4 + 1) * 4, sub * 128:(sub + 1) * 128],
                    pst[:].rearrange("p (c f) -> p c f", f=128)),
                    inc="SV", waits=[("ST", tv)])
            qT_ready = pst_free
        for pp0 in [0]:
            pass
        for h in range(NH):
            g = h // 4
            dsv_last = 0
            for pp in range(16):
                pb = pp % 2
                pq = (h * 16 + pp) % 4
                mm2 = None
                for half in range(2):
                    ci = pp * 2 + half
                    w = []
                    if half == 0:
                        if pp == 0 and h == 0:
                            w += [("SV", qT_ready), negb_w,
                                  ("SV", a2s["add4_prev"])]
                        w.append(("SS", a2s["prev_exp"][pb]))
                    mm2 = c.op("t", lambda e, ci=ci, pb=pb, half=half, h=h, g=g:
                               e.matmul(psB[:, 2 * pb + half, :],
                                        kT[:, g, ci * 128:(ci + 1) * 128],
                                        qT[:, h, :], start=True, stop=True),
                               inc="ST" if half == 1 else None, waits=w)
                if pp == 0 and a2s["pend"] is not None:
                    _emit_attn_epilogue(c, a2s, psR, psO, onesf, drecip, rb, aoT)
                ex = c.op("s", lambda e, pb=pb, pq=pq, h=h: e.activation(
                    Pb[:, pq, :],
                    psB[:, 2 * pb:2 * pb + 2, :].rearrange("p a f -> p (a f)"),
                    AF.Exp, bias=negB[:, h:h + 1], scale=ISR),
                    inc="SS",
                    waits=[("ST", mm2), ("ST", a2s["exp_free"][pq][0]),
                           ("SV", a2s["exp_free"][pq][1])])
                a2s["prev_exp"][pb] = ex
                pv2 = None
                for half in range(2):
                    ci = pp * 2 + half
                    w = []
                    if half == 0:
                        w = [("SS", ex)]
                        if pp == 0:
                            w.append(("SV", a2s["psO_free"]))
                    pv2 = c.op("t", lambda e, ci=ci, pq=pq, half=half, g=g:
                               e.matmul(psO[:],
                                        vst[:, ci, g * 128:(g + 1) * 128],
                                        Pb[:, pq, half * 512:(half + 1) * 512],
                                        start=(ci == 0), stop=(ci == 31)),
                               inc="ST" if (half == 1) else None, waits=w)
                uo = c.op("v", lambda e, pb=pb, pq=pq: e.tensor_add(
                    u2[:, pb, :], Pb[:, pq, 0:512], Pb[:, pq, 512:1024]),
                    inc="SV", waits=[("SS", ex), ("SV", a2s["u_free"])])
                a2s["exp_free"][pq] = (pv2, uo)
                if pb == 1:
                    grp = pp // 2
                    dsv = c.op("v", lambda e, grp=grp: e.tensor_add(
                        dsum[:, grp, :], u2[:, 0, :], u2[:, 1, :]),
                        inc="SV", waits=[("SV", uo), ("ST", a2s["dsum_free"])])
                    a2s["u_free"] = dsv
                    dsv_last = dsv
            den_mm = 0
            for grp in range(8):
                w = []
                if grp == 0:
                    w = [("SV", dsv_last), ("SV", a2s["den_free"])]
                den_mm = c.op("t", lambda e, grp=grp: e.matmul(
                    psDen[:], onesh[:], dsum[:, grp, :],
                    start=(grp == 0), stop=(grp == 7)),
                    inc="ST" if grp == 7 else None, waits=w)
            a2s["dsum_free"] = den_mm
            dcp = c.op("v", lambda e: e.tensor_copy(densb[:], psDen[:]),
                       inc="SV", waits=[("ST", den_mm)])
            a2s["den_free"] = dcp
            rc = c.op("v", lambda e: e.reciprocal(drecip[:], densb[:]),
                      inc="SV", waits=[("SV", dcp)])
            a2s["pend"] = (h, rc, c.val("ST"))
        _emit_attn_epilogue(c, a2s, psR, psO, onesf, drecip, rb, aoT)
        ao_done = a2s["ao_done"]
        for sub in range(4):
            b = sub % 2
            gi = qt * 4 + sub
            dxr = c.dma("y", xr[b][:], x_in[gi * 128:(gi + 1) * 128, :],
                        "Dxr0", waits=[xr_free[0]])
            tv = None
            for hc in range(DC):
                for n in range(4):
                    w = []
                    if hc == 0 and n == 0:
                        w = [("SV", ao_done), dwo]
                        if add4:
                            w.append(("SV", add4))
                    tv = c.op("t", lambda e, hc=hc, n=n, sub=sub: e.matmul(
                        psB[:, n, :], aoT[:, hc, sub * 128:(sub + 1) * 128],
                        wos[:, hc, n * 512:(n + 1) * 512],
                        start=(hc == 0), stop=(hc == DC - 1)),
                        inc="ST" if (hc == DC - 1 and n == 3) else None, waits=w)
            wo_last = tv
            for n in range(4):
                add4 = c.op("v", lambda e, n=n, b=b: e.tensor_tensor(
                    xr[b][:, n * 512:(n + 1) * 512], psB[:, n, :],
                    xr[b][:, n * 512:(n + 1) * 512], op=ALU.add),
                    inc="SV", waits=([("ST", wo_last), dxr]
                                     if n == 0 else ()))
            xr_free[0] = c.dma("y", x_out[gi * 128:(gi + 1) * 128, :], xr[b][:],
                               "Dso0", waits=[("SV", add4)])
        wo_done = wo_last
        a2s["add4_prev"] = add4
        c.barrier()
    a2_es.close()
    pa_es.close()


def _rope_tables():
    inv = (1.0 / (10000.0 ** (np.arange(0, 64, 2, dtype=np.float32)
                              / np.float32(64)))).astype(np.float32)
    ang = np.arange(S, dtype=np.float32)[:, None] * inv[None, :]
    cos = np.cos(ang).astype(np.float32)
    sin = np.sin(ang).astype(np.float32)
    return np.tile(cos, (1, 16)), np.tile(sin, (1, 16))


def _perm_cols(w, nheads):
    perm = np.concatenate([np.arange(0, 64, 2), np.arange(1, 64, 2),
                           np.arange(64, 128)])
    w = np.asarray(w, dtype=np.float32).copy()
    out = np.empty_like(w)
    for h in range(nheads):
        out[:, h * 128:(h + 1) * 128] = w[:, h * 128 + perm]
    return out


def run_phase_a(x, Wq, Wk, Wv, Wo, debug=0):
    nc = bass.Bass("TRN2", target_bir_lowering=False)
    build_phase_a(nc, debug=debug)
    wq = np.ascontiguousarray(_perm_cols(Wq, 16))
    wk = np.ascontiguousarray(_perm_cols(Wk, 4))
    wv = np.ascontiguousarray(np.asarray(Wv, dtype=np.float32))
    wo = np.ascontiguousarray(np.asarray(Wo, dtype=np.float32))
    cosq, sinq = _rope_tables()
    map16 = np.zeros((4, 16), np.float32)
    for h in range(16):
        map16[h // 4, h] = 1.0
    in_maps = [dict(x_in=np.ascontiguousarray(x[i]), wq=wq, wk=wk, wv=wv, wo=wo,
                    cosq=cosq, sinq=sinq, map16=map16, identh=_ident_f16(),
                    identf=np.eye(128, dtype=np.float32),
                    onesf=_onesf()) for i in range(NCORE)]
    res = run_bass_kernel_spmd(nc, in_maps, core_ids=list(range(NCORE)))
    return np.stack([r["x_out"] for r in res.results], axis=0)



def build_phase_c(nc):
    c = Ctx(nc)
    x_in = nc.dram_tensor("x_in", [S, D], F32, kind="ExternalInput").ap()
    w_hg = nc.dram_tensor("w_hg", [D, 6144], F32, kind="ExternalInput").ap()
    w_out = nc.dram_tensor("w_out", [3072, D], F32, kind="ExternalInput").ap()
    ident_in = nc.dram_tensor("ident", [128, 128], F16, kind="ExternalInput").ap()
    x_out = nc.dram_tensor("x_out", [S, D], F32, kind="ExternalOutput").ap()
    emit_phase_c(c, nc, x_in, x_out, w_hg, w_out, ident_in)
    c.finalize()
    return nc


def emit_phase_c(c, nc, x_in, x_out, w_hg, w_out, ident_in):
    EC = 48
    ECH = 24
    DC = 16
    TT = 1024
    NST = S // TT
    NSUB = TT // 128
    es = ExitStack()
    def sb(name, shape, dtype):
        return es.enter_context(nc.sbuf_tensor("c_" + name, shape, dtype))
    def ps(name, shape, dtype=F32):
        return es.enter_context(nc.psum_tensor("c_" + name, shape, dtype))

    whg_t4 = c.dram("whg_t4", [EC, 128, DC, 128], F16).ap()
    wout_t4 = c.dram("wout_t4", [DC, 128, ECH, 128], F16).ap()

    ident = sb("ident_s", [128, 128], F16)
    pst = ps("pst", [128, 512], F16)
    pstb = ps("pstb", [128, 512], F16)
    psC = ps("psC", [128, 2, TT], F32)

    c.dma("y", ident[:], ident_in, "cDc")

    prep_es = ExitStack()
    wrow = [prep_es.enter_context(nc.sbuf_tensor(f"c_wrow{i}", [128, 6144], F16))
            for i in range(2)]
    gw = [("Dpr0", 0), ("Dpr1", 0)]
    for dc in range(DC):
        b = dc % 2
        dl = c.dma("g", wrow[b][:], w_hg[dc * 128:(dc + 1) * 128, :], f"Dpr{b}",
                   waits=[gw[b]])
        gw[b] = c.dma("g", whg_t4[:, :, dc, :].rearrange("e p f -> p e f"),
                      wrow[b][:], f"Dpr{b}", waits=[dl])
    for ec in range(ECH):
        b = ec % 2
        dl = c.dma("g", wrow[b][:, 0:2048], w_out[ec * 128:(ec + 1) * 128, :],
                   f"Dpr{b}", waits=[gw[b]])
        gw[b] = c.dma("g", wout_t4[:, :, ec, :].rearrange("d p f -> p d f"),
                      wrow[b][:, 0:2048], f"Dpr{b}", waits=[dl])
    c.barrier()
    prep_es.close()

    xf = [sb(f"xf{i}", [128, D], F32) for i in range(2)]
    xnb = sb("xnb", [128, D], F16)
    xnTo = sb("xnTo", [128, DC, TT], F16)
    hb = sb("hb", [128, ECH, TT], F16)
    wgslab = [sb(f"wgslab{i}", [128, DC, 128], F16) for i in range(2)]
    whslab = [sb(f"whslab{i}", [128, DC, 128], F16) for i in range(2)]
    woslab = [sb(f"woslab{i}", [128, ECH, 128], F16) for i in range(2)]
    a_s = sb("a_s", [128, TT], F32)
    z_s = sb("z_s", [128, TT], F32)
    s_s = sb("s_s", [128, TT], F32)
    r_s = sb("r_s", [128, TT], F32)
    t_s = sb("t_s", [128, TT], F32)
    b_s = sb("b_s", [128, TT], F32)
    h_s = sb("h_s", [128, TT], F32)
    carry = sb("carry", [128, ECH], F32)
    xo = [sb(f"xo{i}", [128, D], F32) for i in range(2)]
    ln = LNHelper(c, "c", alloc=sb)

    xf_free = [0, 0]
    xo_free = [("Dso0", 0), ("Dso1", 0)]
    for st in range(NST):
        pst_free = 0
        tlast = 0
        for sub in range(NSUB):
            gi = st * NSUB + sub
            b = sub % 2
            dy = c.dma("y", xf[b][:], x_in[gi * 128:(gi + 1) * 128, :],
                       f"Dx{b}", waits=[("SV", xf_free[b])])
            lnw = ln.emit(xf[b][:], xnb[:], [dy],
                          extra_waits=[("ST", tlast)])
            xf_free[b] = lnw[1]
            for g4 in range(4):
                tv = None
                for i in range(g4 * 4, g4 * 4 + 4):
                    w = [lnw, ("SV", pst_free)] if i == g4 * 4 else ()
                    tv = c.op("t", lambda e, i=i, g4=g4: e.transpose(
                        pst[:, (i - g4 * 4) * 128:(i - g4 * 4 + 1) * 128],
                        xnb[:, i * 128:(i + 1) * 128], ident[:]),
                        inc="ST" if i == g4 * 4 + 3 else None, waits=w)
                tlast = tv
                pst_free = c.op("v", lambda e, g4=g4, sub=sub: e.tensor_copy(
                    xnTo[:, g4 * 4:(g4 + 1) * 4, sub * 128:(sub + 1) * 128],
                    pst[:].rearrange("p (c f) -> p c f", f=128)),
                    inc="SV", waits=[("ST", tv)])
        c.barrier()

        wg_free = [0, 0]
        wh_free = [0, 0]
        ps_free = [("SV", 0), ("SV", 0)]
        hb_cast = 0
        for ec in range(ECH):
            b = ec % 2
            dyg = c.dma("y", wgslab[b][:], whg_t4[ECH + ec], f"Dwg{b}",
                        waits=[("ST", wg_free[b])])
            dyh = c.dma("y", whslab[b][:], whg_t4[ec], f"Dwh{b}",
                        waits=[("ST", wh_free[b])])
            tvg = None
            for dc in range(DC):
                for nt in range(TT // 512):
                    w = ([dyg, ps_free[0]]
                         if (dc == 0 and nt == 0) else ())
                    tvg = c.op("t", lambda e, dc=dc, nt=nt, b=b: e.matmul(
                        psC[:, 0, nt * 512:(nt + 1) * 512], wgslab[b][:, dc, :],
                        xnTo[:, dc, nt * 512:(nt + 1) * 512],
                        start=(dc == 0), stop=(dc == DC - 1)),
                        inc="ST" if (dc == DC - 1 and nt == TT // 512 - 1) else None,
                        waits=w)
            wg_free[b] = tvg
            tvh = None
            for dc in range(DC):
                for nt in range(TT // 512):
                    w = ([dyh, ps_free[1]]
                         if (dc == 0 and nt == 0) else ())
                    tvh = c.op("t", lambda e, dc=dc, nt=nt, b=b: e.matmul(
                        psC[:, 1, nt * 512:(nt + 1) * 512], whslab[b][:, dc, :],
                        xnTo[:, dc, nt * 512:(nt + 1) * 512],
                        start=(dc == 0), stop=(dc == DC - 1)),
                        inc="ST" if (dc == DC - 1 and nt == TT // 512 - 1) else None,
                        waits=w)
            wh_free[b] = tvh
            sa = c.op("s", lambda e: e.activation(a_s[:], psC[:, 0, :], AF.Sigmoid,
                                                  scale=-1.0),
                      inc="SS", waits=[("ST", tvg), ("SV", hb_cast)])
            vz = c.op("v", lambda e: e.tensor_scalar(z_s[:], a_s[:], -1.0, 1.0,
                                                     op0=ALU.mult, op1=ALU.add),
                      inc="SV", waits=[("SS", sa)])
            ss = c.op("s", lambda e: e.activation(s_s[:], psC[:, 1, :], AF.Sigmoid),
                      inc="SS", waits=[("ST", tvh), ("SV", hb_cast)])
            vr = c.op("v", lambda e: e.tensor_scalar(r_s[:], psC[:, 1, :], 0.0, None,
                                                     op0=ALU.max),
                      inc="SV", waits=[("ST", tvh), ("SV", hb_cast), ("SS", ss)])
            ps_free[1] = ("SV", vr)
            ps_free[0] = ("SS", sa)
            vt = c.op("v", lambda e: e.scalar_tensor_tensor(
                t_s[:], s_s[:], 0.5, r_s[:], op0=ALU.min, op1=ALU.add),
                inc="SV", waits=[("SS", ss), ("SV", vr)])
            vb = c.op("v", lambda e: e.tensor_tensor(b_s[:], t_s[:], z_s[:],
                                                     op=ALU.mult),
                      inc="SV", waits=[("SV", vt), ("SV", vz)])
            init = 0.0 if st == 0 else carry[:, ec:ec + 1]
            vh = c.op("v", lambda e, init=init: e.tensor_tensor_scan(
                h_s[:], a_s[:], b_s[:], init, ALU.mult, ALU.add),
                inc="SV", waits=[("SV", vb)])
            vc = c.op("v", lambda e, ec=ec: e.tensor_copy(carry[:, ec:ec + 1],
                                                          h_s[:, TT - 1:TT]),
                      inc="SV", waits=[("SV", vh)])
            hb_cast = c.op("v", lambda e, ec=ec: e.tensor_copy(hb[:, ec, :], h_s[:]),
                           inc="SV", waits=[("SV", vc)])
        c.barrier()

        wo_free = [0, 0]
        ps_free = [0, 0]
        for dc in range(DC):
            b = dc % 2
            dy = c.dma("y", woslab[b][:], wout_t4[dc], f"Dwo{b}",
                       waits=[("ST", wo_free[b])])
            tv = None
            for ec in range(ECH):
                for nt in range(TT // 512):
                    w = ([dy, ("SV", ps_free[b])]
                         if (ec == 0 and nt == 0) else ())
                    tv = c.op("t", lambda e, ec=ec, nt=nt, b=b: e.matmul(
                        psC[:, b, nt * 512:(nt + 1) * 512], woslab[b][:, ec, :],
                        hb[:, ec, nt * 512:(nt + 1) * 512],
                        start=(ec == 0), stop=(ec == ECH - 1)),
                        inc="ST" if (ec == ECH - 1 and nt == TT // 512 - 1) else None,
                        waits=w)
            wo_free[b] = tv
            ps_free[b] = c.op("v", lambda e, b=b, dc=dc: e.tensor_copy(
                xnTo[:, dc, :], psC[:, b, :]),
                inc="SV", waits=[("ST", tv)])
        c.barrier()

        pstb_free = 0
        for tsub in range(NSUB):
            gi = st * NSUB + tsub
            b = tsub % 2
            dy = c.dma("y", xo[b][:], x_in[gi * 128:(gi + 1) * 128, :],
                       f"Dxo{b}", waits=[xo_free[b]])
            add_last = 0
            for g4 in range(4):
                tv = None
                for i in range(g4 * 4, g4 * 4 + 4):
                    dc = i
                    w = [("SV", pstb_free)] if i == g4 * 4 else ()
                    tv = c.op("t", lambda e, dc=dc, g4=g4, tsub=tsub: e.transpose(
                        pstb[:, (dc - g4 * 4) * 128:(dc - g4 * 4 + 1) * 128],
                        xnTo[:, dc, tsub * 128:(tsub + 1) * 128], ident[:]),
                        inc="ST" if i == g4 * 4 + 3 else None, waits=w)
                add_last = c.op("v", lambda e, g4=g4, b=b: e.tensor_tensor(
                    xo[b][:, g4 * 512:(g4 + 1) * 512], pstb[:],
                    xo[b][:, g4 * 512:(g4 + 1) * 512], op=ALU.add),
                    inc="SV", waits=[("ST", tv), dy])
                pstb_free = add_last
            xo_free[b] = c.dma("y", x_out[gi * 128:(gi + 1) * 128, :], xo[b][:],
                               f"Dso{b}", waits=[("SV", add_last)])
        c.barrier()
    es.close()



def build_phase_d(nc):
    c = Ctx(nc)
    x_in = nc.dram_tensor("x_in", [S, D], F32, kind="ExternalInput").ap()
    giT = nc.dram_tensor("giT", [D, 5504], I32, kind="ExternalInput").ap()
    uiT = nc.dram_tensor("uiT", [D, 5504], I32, kind="ExternalInput").ap()
    diT = nc.dram_tensor("diT", [5504, D], I32, kind="ExternalInput").ap()
    lut_in = nc.dram_tensor("lut", [1, 16], F32, kind="ExternalInput").ap()
    ident_in = nc.dram_tensor("ident", [128, 128], F16, kind="ExternalInput").ap()
    onesf_in = nc.dram_tensor("onesf", [128, 128], F32, kind="ExternalInput").ap()
    x_out = nc.dram_tensor("x_out", [S, D], F32, kind="ExternalOutput").ap()
    emit_phase_d(c, nc, x_in, x_out, giT, uiT, diT, lut_in, ident_in, onesf_in)
    c.finalize()
    return nc


def emit_phase_d(c, nc, x_in, x_out, giT, uiT, diT, lut_in, ident_in, onesf_in):
    OC5 = 43
    DC = 16
    INTER = 5504
    TT = 1024
    NST = S // TT
    NSUB = TT // 128
    es = ExitStack()
    def sb(name, shape, dtype):
        return es.enter_context(nc.sbuf_tensor("d_" + name, shape, dtype))
    def ps(name, shape, dtype=F32):
        return es.enter_context(nc.psum_tensor("d_" + name, shape, dtype))

    wg_t4 = c.dram("wg_t4", [OC5, 128, DC, 128], F16).ap()
    wu_t4 = c.dram("wu_t4", [OC5, 128, DC, 128], F16).ap()
    wd_t4 = c.dram("wd_t4", [DC, 128, OC5, 128], F16).ap()

    ident = sb("ident_s", [128, 128], F16)
    onesf = sb("onesf_s", [128, 128], F32)
    luts = sb("luts", [1, 16], F32)
    lutb = sb("lutb", [128, 16], F32)
    pst = ps("pst", [128, 512], F16)
    pstb = ps("pstb", [128, 512], F16)
    psA = ps("psA", [128, 2, TT], F32)
    psb3 = ps("psb3", [128, 16], F32)

    c.dma("y", ident[:], ident_in, "dDc")
    dy0 = c.dma("y", onesf[:], onesf_in, "dDc")
    dy1 = c.dma("y", luts[:], lut_in, "dDc")
    t0 = c.op("t", lambda e: e.matmul(psb3[:], onesf[0:1, :], luts[:], start=True,
                                      stop=True),
              inc="ST", waits=[dy1])
    lb = c.op("v", lambda e: e.tensor_copy(lutb[:], psb3[:]), inc="SV",
              waits=[("ST", t0)])

    prep_es = ExitStack()
    idxs = [prep_es.enter_context(nc.sbuf_tensor(f"d_idxs{i}", [128, INTER], I32))
            for i in range(2)]
    idxf = prep_es.enter_context(nc.sbuf_tensor("d_idxf", [128, INTER], F16))
    tmpb = prep_es.enter_context(nc.sbuf_tensor("d_tmpb", [128, INTER], F16))
    accb = prep_es.enter_context(nc.sbuf_tensor("d_accb", [128, INTER], F16))

    gw = [0, 0]
    vprev = 0
    wprev = ("Dgw", 0)
    chunk_list = ([("g", hc) for hc in range(DC)] + [("u", hc) for hc in range(DC)]
                  + [("d", oc) for oc in range(OC5)])
    for kind, ci in chunk_list:
        b = ci % 2
        cols = INTER if kind in ("g", "u") else D
        src = {"g": giT, "u": uiT, "d": diT}[kind]
        dl = c.dma("g", idxs[b][:, 0:cols], src[ci * 128:(ci + 1) * 128, :],
                   f"Dpi{b}", waits=[("SV", gw[b])])
        cv = c.op("v", lambda e, b=b, cols=cols: e.tensor_copy(
            idxf[:, 0:cols], idxs[b][:, 0:cols]),
            inc="SV", waits=[dl, ("SV", vprev)])
        gw[b] = cv
        hc_ = cols // 2
        rngs = [(0, hc_), (hc_, cols)]
        accv = [None, None]
        for v in range(16):
            tvs = [None, None]
            for hi, (a0, a1) in enumerate(rngs):
                if v == 0:
                    accv[hi] = c.op("v", lambda e, a0=a0, a1=a1, v=v:
                                    e.tensor_scalar(accb[:, a0:a1],
                                                    idxf[:, a0:a1], float(v),
                                                    lutb[:, v:v + 1],
                                                    op0=ALU.is_equal,
                                                    op1=ALU.mult),
                                    inc="SV",
                                    waits=[("SV", cv), ("SV", lb), wprev])
                else:
                    tvs[hi] = c.op("v", lambda e, a0=a0, a1=a1, v=v:
                                   e.tensor_scalar(tmpb[:, a0:a1],
                                                   idxf[:, a0:a1], float(v),
                                                   lutb[:, v:v + 1],
                                                   op0=ALU.is_equal,
                                                   op1=ALU.mult),
                                   inc="SV", waits=[("SV", accv[hi])])
            if v > 0:
                for hi, (a0, a1) in enumerate(rngs):
                    accv[hi] = c.op("v", lambda e, a0=a0, a1=a1: e.tensor_tensor(
                        accb[:, a0:a1], accb[:, a0:a1], tmpb[:, a0:a1],
                        op=ALU.add),
                        inc="SV", waits=[("SV", tvs[hi])])
        vprev = max(accv)
        if kind == "g":
            wprev = c.dma("g", wg_t4[:, :, ci, :].rearrange("o p f -> p o f"),
                          accb[:], "Dgw", waits=[("SV", vprev)])
        elif kind == "u":
            wprev = c.dma("g", wu_t4[:, :, ci, :].rearrange("o p f -> p o f"),
                          accb[:], "Dgw", waits=[("SV", vprev)])
        else:
            wprev = c.dma("g", wd_t4[:, :, ci, :].rearrange("d p f -> p d f"),
                          accb[:, 0:D], "Dgw", waits=[("SV", vprev)])
    c.barrier()
    prep_es.close()

    xf = [sb(f"xf{i}", [128, D], F32) for i in range(2)]
    xnb = sb("xnb", [128, D], F16)
    xnTo = sb("xnTo", [128, DC, TT], F16)
    hmb = sb("hmb", [128, OC5, TT], F16)
    wgslab = [sb(f"wgslab{i}", [128, DC, 128], F16) for i in range(2)]
    wuslab = [sb(f"wuslab{i}", [128, DC, 128], F16) for i in range(2)]
    wdslab = [sb(f"wdslab{i}", [128, OC5, 128], F16) for i in range(2)]
    sil = sb("sil", [128, TT], F32)
    xo = [sb(f"xo{i}", [128, D], F32) for i in range(2)]
    ln = LNHelper(c, "d", alloc=sb)

    xf_free = [0, 0]
    xo_free = [("Dso0", 0), ("Dso1", 0)]
    for st in range(NST):
        pst_free = 0
        tlast = 0
        for sub in range(NSUB):
            gi = st * NSUB + sub
            b = sub % 2
            dy = c.dma("y", xf[b][:], x_in[gi * 128:(gi + 1) * 128, :],
                       f"Dx{b}", waits=[("SV", xf_free[b])])
            lnw = ln.emit(xf[b][:], xnb[:], [dy],
                          extra_waits=[("ST", tlast)])
            xf_free[b] = lnw[1]
            for g4 in range(4):
                tv = None
                for i in range(g4 * 4, g4 * 4 + 4):
                    w = [lnw, ("SV", pst_free)] if i == g4 * 4 else ()
                    tv = c.op("t", lambda e, i=i, g4=g4: e.transpose(
                        pst[:, (i - g4 * 4) * 128:(i - g4 * 4 + 1) * 128],
                        xnb[:, i * 128:(i + 1) * 128], ident[:]),
                        inc="ST" if i == g4 * 4 + 3 else None, waits=w)
                tlast = tv
                pst_free = c.op("v", lambda e, g4=g4, sub=sub: e.tensor_copy(
                    xnTo[:, g4 * 4:(g4 + 1) * 4, sub * 128:(sub + 1) * 128],
                    pst[:].rearrange("p (c f) -> p c f", f=128)),
                    inc="SV", waits=[("ST", tv)])
        c.barrier()

        wg_free = [0, 0]
        wu_free = [0, 0]
        ps_free = [("SV", 0), ("SV", 0)]
        sil_free = 0
        for oc in range(OC5):
            b = oc % 2
            dyg = c.dma("y", wgslab[b][:], wg_t4[oc], f"Dwg{b}",
                        waits=[("ST", wg_free[b])])
            dyu = c.dma("y", wuslab[b][:], wu_t4[oc], f"Dwu{b}",
                        waits=[("ST", wu_free[b])])
            tvg = None
            for dc in range(DC):
                for nt in range(TT // 512):
                    w = ([dyg, ps_free[0]]
                         if (dc == 0 and nt == 0) else ())
                    tvg = c.op("t", lambda e, dc=dc, nt=nt, b=b: e.matmul(
                        psA[:, 0, nt * 512:(nt + 1) * 512], wgslab[b][:, dc, :],
                        xnTo[:, dc, nt * 512:(nt + 1) * 512],
                        start=(dc == 0), stop=(dc == DC - 1)),
                        inc="ST" if (dc == DC - 1 and nt == TT // 512 - 1) else None,
                        waits=w)
            wg_free[b] = tvg
            tvu = None
            for dc in range(DC):
                for nt in range(TT // 512):
                    w = ([dyu, ps_free[1]]
                         if (dc == 0 and nt == 0) else ())
                    tvu = c.op("t", lambda e, dc=dc, nt=nt, b=b: e.matmul(
                        psA[:, 1, nt * 512:(nt + 1) * 512], wuslab[b][:, dc, :],
                        xnTo[:, dc, nt * 512:(nt + 1) * 512],
                        start=(dc == 0), stop=(dc == DC - 1)),
                        inc="ST" if (dc == DC - 1 and nt == TT // 512 - 1) else None,
                        waits=w)
            wu_free[b] = tvu
            sv = c.op("s", lambda e: e.activation(sil[:], psA[:, 0, :], AF.Silu),
                      inc="SS", waits=[("ST", tvg), ("SV", sil_free)])
            ps_free[0] = ("SS", sv)
            hm = c.op("v", lambda e, oc=oc: e.tensor_tensor(
                hmb[:, oc, :], sil[:], psA[:, 1, :], op=ALU.mult),
                inc="SV", waits=[("SS", sv), ("ST", tvu)])
            ps_free[1] = ("SV", hm)
            sil_free = hm
        c.barrier()

        wd_free = [0, 0]
        ps_free = [0, 0]
        for dc in range(DC):
            b = dc % 2
            dy = c.dma("y", wdslab[b][:], wd_t4[dc], f"Dwd{b}",
                       waits=[("ST", wd_free[b])])
            tv = None
            for oc in range(OC5):
                for nt in range(TT // 512):
                    w = ([dy, ("SV", ps_free[b])]
                         if (oc == 0 and nt == 0) else ())
                    tv = c.op("t", lambda e, oc=oc, nt=nt, b=b: e.matmul(
                        psA[:, b, nt * 512:(nt + 1) * 512], wdslab[b][:, oc, :],
                        hmb[:, oc, nt * 512:(nt + 1) * 512],
                        start=(oc == 0), stop=(oc == OC5 - 1)),
                        inc="ST" if (oc == OC5 - 1 and nt == TT // 512 - 1) else None,
                        waits=w)
            wd_free[b] = tv
            ps_free[b] = c.op("v", lambda e, b=b, dc=dc: e.tensor_copy(
                xnTo[:, dc, :], psA[:, b, :]),
                inc="SV", waits=[("ST", tv)])
        c.barrier()

        pstb_free = 0
        for tsub in range(NSUB):
            gi = st * NSUB + tsub
            b = tsub % 2
            dy = c.dma("y", xo[b][:], x_in[gi * 128:(gi + 1) * 128, :],
                       f"Dxo{b}", waits=[xo_free[b]])
            add_last = 0
            for g4 in range(4):
                tv = None
                for i in range(g4 * 4, g4 * 4 + 4):
                    dc = i
                    w = [("SV", pstb_free)] if i == g4 * 4 else ()
                    tv = c.op("t", lambda e, dc=dc, g4=g4, tsub=tsub: e.transpose(
                        pstb[:, (dc - g4 * 4) * 128:(dc - g4 * 4 + 1) * 128],
                        xnTo[:, dc, tsub * 128:(tsub + 1) * 128], ident[:]),
                        inc="ST" if i == g4 * 4 + 3 else None, waits=w)
                add_last = c.op("v", lambda e, g4=g4, b=b: e.tensor_tensor(
                    xo[b][:, g4 * 512:(g4 + 1) * 512], pstb[:],
                    xo[b][:, g4 * 512:(g4 + 1) * 512], op=ALU.add),
                    inc="SV", waits=[("ST", tv), dy])
                pstb_free = add_last
            xo_free[b] = c.dma("y", x_out[gi * 128:(gi + 1) * 128, :], xo[b][:],
                               f"Dso{b}", waits=[("SV", add_last)])
        c.barrier()
    es.close()


def run_phase_c(x, w_hg, w_out):
    nc = bass.Bass("TRN2", target_bir_lowering=False)
    build_phase_c(nc)
    w_hg = np.ascontiguousarray(np.asarray(w_hg, dtype=np.float32))
    w_out = np.ascontiguousarray(np.asarray(w_out, dtype=np.float32))
    ident = _ident_f16()
    in_maps = [dict(x_in=np.ascontiguousarray(x[i]), w_hg=w_hg, w_out=w_out,
                    ident=ident) for i in range(NCORE)]
    res = run_bass_kernel_spmd(nc, in_maps, core_ids=list(range(NCORE)))
    return np.stack([r["x_out"] for r in res.results], axis=0)


def run_phase_d(x, lut, gate_idx, up_idx, down_idx):
    nc = bass.Bass("TRN2", target_bir_lowering=False)
    build_phase_d(nc)
    giT = np.ascontiguousarray(np.asarray(gate_idx).T.astype(np.int32))
    uiT = np.ascontiguousarray(np.asarray(up_idx).T.astype(np.int32))
    diT = np.ascontiguousarray(np.asarray(down_idx).T.astype(np.int32))
    lut2 = np.asarray(lut, dtype=np.float32).reshape(1, 16)
    ident = _ident_f16()
    onesf = _onesf()
    in_maps = [dict(x_in=np.ascontiguousarray(x[i]), giT=giT, uiT=uiT, diT=diT,
                    lut=lut2, ident=ident, onesf=onesf) for i in range(NCORE)]
    res = run_bass_kernel_spmd(nc, in_maps, core_ids=list(range(NCORE)))
    return np.stack([r["x_out"] for r in res.results], axis=0)



def _ident_bf16():
    import ml_dtypes
    return np.eye(128, dtype=ml_dtypes.bfloat16)


def _ident_f16():
    return np.eye(128, dtype=np.float16)


def _onesf():
    return np.ones((128, 128), dtype=np.float32)


def run_phase_b(x, w_fc, w_proj):
    nc = bass.Bass("TRN2", target_bir_lowering=False)
    build_phase_b(nc)
    wfcT = np.ascontiguousarray(np.asarray(w_fc).T)
    wprojT = np.ascontiguousarray(np.asarray(w_proj).T)
    ident = _ident_bf16()
    onesf = _onesf()
    in_maps = [dict(x_in=np.ascontiguousarray(x[i]), wfcT=wfcT, wprojT=wprojT,
                    ident=ident, onesf=onesf) for i in range(NCORE)]
    res = run_bass_kernel_spmd(nc, in_maps, core_ids=list(range(NCORE)))
    return np.stack([r["x_out"] for r in res.results], axis=0)


def build_fused(nc):
    c = Ctx(nc)
    x_in = nc.dram_tensor("x_in", [S, D], F32, kind="ExternalInput").ap()
    wq_in = nc.dram_tensor("wq", [D, D], F32, kind="ExternalInput").ap()
    wk_in = nc.dram_tensor("wk", [D, 512], F32, kind="ExternalInput").ap()
    wv_in = nc.dram_tensor("wv", [D, 512], F32, kind="ExternalInput").ap()
    wo_in = nc.dram_tensor("wo", [D, D], F32, kind="ExternalInput").ap()
    cos_in = nc.dram_tensor("cosq", [S, 512], F32, kind="ExternalInput").ap()
    sin_in = nc.dram_tensor("sinq", [S, 512], F32, kind="ExternalInput").ap()
    map_in = nc.dram_tensor("map16", [4, 16], F32, kind="ExternalInput").ap()
    identh_in = nc.dram_tensor("identh", [128, 128], F16, kind="ExternalInput").ap()
    identf_in = nc.dram_tensor("identf", [128, 128], F32, kind="ExternalInput").ap()
    onesf_in = nc.dram_tensor("onesf", [128, 128], F32, kind="ExternalInput").ap()
    identb_in = nc.dram_tensor("identb", [128, 128], BF16, kind="ExternalInput").ap()
    wfcT = nc.dram_tensor("wfcT", [D, 5120], F32, kind="ExternalInput").ap()
    wprojT = nc.dram_tensor("wprojT", [5120, D], F32, kind="ExternalInput").ap()
    whg_in = nc.dram_tensor("w_hg", [D, 6144], F32, kind="ExternalInput").ap()
    wout_in = nc.dram_tensor("w_out", [3072, D], F32, kind="ExternalInput").ap()
    giT = nc.dram_tensor("giT", [D, 5504], I32, kind="ExternalInput").ap()
    uiT = nc.dram_tensor("uiT", [D, 5504], I32, kind="ExternalInput").ap()
    diT = nc.dram_tensor("diT", [5504, D], I32, kind="ExternalInput").ap()
    lut_in = nc.dram_tensor("lut", [1, 16], F32, kind="ExternalInput").ap()
    x_out = nc.dram_tensor("x_out", [S, D], F32, kind="ExternalOutput").ap()

    x1 = c.dram("x1", [S, D], F32).ap()
    x2 = c.dram("x2", [S, D], F32).ap()
    x3 = c.dram("x3", [S, D], F32).ap()

    emit_phase_a(c, nc, x_in, x1, wq_in, wk_in, wv_in, wo_in, cos_in, sin_in,
                 map_in, identh_in, identf_in, onesf_in)
    emit_phase_b(c, nc, x1, x2, wfcT, wprojT, identb_in, onesf_in)
    emit_phase_c(c, nc, x2, x3, whg_in, wout_in, identh_in)
    emit_phase_d(c, nc, x3, x_out, giT, uiT, diT, lut_in, identh_in, onesf_in)
    c.finalize()
    return nc


def fused_in_maps(inputs):
    x = np.asarray(inputs["L0_attn_hidden_states"], dtype=np.float32)
    wq = np.ascontiguousarray(_perm_cols(inputs["L0_attn_Wq"], 16))
    wk = np.ascontiguousarray(_perm_cols(inputs["L0_attn_Wk"], 4))
    wv = np.ascontiguousarray(np.asarray(inputs["L0_attn_Wv"], dtype=np.float32))
    wo = np.ascontiguousarray(np.asarray(inputs["L0_attn_Wo"], dtype=np.float32))
    cosq, sinq = _rope_tables()
    map16 = np.zeros((4, 16), np.float32)
    for h in range(16):
        map16[h // 4, h] = 1.0
    wfcT = np.ascontiguousarray(np.asarray(inputs["L0_ffn_w_fc"]).T.astype(np.float32))
    wprojT = np.ascontiguousarray(np.asarray(inputs["L0_ffn_w_proj"]).T.astype(np.float32))
    w_hg = np.ascontiguousarray(np.asarray(inputs["L1_attn_w_hg"], dtype=np.float32))
    w_out = np.ascontiguousarray(np.asarray(inputs["L1_attn_w_out"], dtype=np.float32))
    giT = np.ascontiguousarray(np.asarray(inputs["L1_ffn_gate_idx"]).T.astype(np.int32))
    uiT = np.ascontiguousarray(np.asarray(inputs["L1_ffn_up_idx"]).T.astype(np.int32))
    diT = np.ascontiguousarray(np.asarray(inputs["L1_ffn_down_idx"]).T.astype(np.int32))
    lut2 = np.asarray(inputs["L1_ffn_lut"], dtype=np.float32).reshape(1, 16)
    shared = dict(wq=wq, wk=wk, wv=wv, wo=wo, cosq=cosq, sinq=sinq, map16=map16,
                  identh=_ident_f16(), identf=np.eye(128, dtype=np.float32),
                  onesf=_onesf(), identb=_ident_bf16(), wfcT=wfcT, wprojT=wprojT,
                  w_hg=w_hg, w_out=w_out, giT=giT, uiT=uiT, diT=diT, lut=lut2)
    return [dict(shared, x_in=np.ascontiguousarray(x[i])) for i in range(NCORE)]


def kernel(**inputs):
    nc = bass.Bass("TRN2", target_bir_lowering=False)
    build_fused(nc)
    in_maps = fused_in_maps(inputs)
    res = run_bass_kernel_spmd(nc, in_maps, core_ids=list(range(NCORE)))
    return np.stack([r["x_out"] for r in res.results], axis=0)


def kernel_unfused(**inputs):
    
    x = np.asarray(inputs["L0_attn_hidden_states"], dtype=np.float32)
    x = run_phase_a(x, np.asarray(inputs["L0_attn_Wq"]),
                    np.asarray(inputs["L0_attn_Wk"]),
                    np.asarray(inputs["L0_attn_Wv"]),
                    np.asarray(inputs["L0_attn_Wo"]))
    x = run_phase_b(x, np.asarray(inputs["L0_ffn_w_fc"]),
                    np.asarray(inputs["L0_ffn_w_proj"]))
    x = run_phase_c(x, np.asarray(inputs["L1_attn_w_hg"]),
                    np.asarray(inputs["L1_attn_w_out"]))
    x = run_phase_d(x, np.asarray(inputs["L1_ffn_lut"]),
                    np.asarray(inputs["L1_ffn_gate_idx"]),
                    np.asarray(inputs["L1_ffn_up_idx"]),
                    np.asarray(inputs["L1_ffn_down_idx"]))
    return x
